# version 45
# baseline (speedup 1.0000x reference)
import math
import numpy as np
import ml_dtypes
import concourse.bass as bass
import concourse.mybir as mybir
from concourse.bass_utils import run_bass_kernel_spmd
from contextlib import ExitStack

F32 = mybir.dt.float32
BF16 = mybir.dt.bfloat16
I32 = mybir.dt.int32
I8 = mybir.dt.int8
AF = mybir.ActivationFunctionType
ALU = mybir.AluOpType
AX = mybir.AxisListType
BF = ml_dtypes.bfloat16

T = 2048
D = 1024
HID = 3072
NH = 16
TTN = 4
TTS = 512
SCALE = 0.125
DQ = 1.0 / 127.0
LN127SQ = 2.0 * math.log(127.0)
SEM_ROT = 20000
G_SERIAL = True
D_SERIAL = False


class Sched:

    def __init__(self, nc, es):
        self.nc, self.es = nc, es
        self.sems = []
        self.steps = []
        self.ch = {}
        self.bgset = set()
        self.last_eng = {}
        self._chan("main")

    def _new_sem(self):
        s = self.es.enter_context(self.nc.semaphore(f"g{len(self.sems)}"))
        self.sems.append(s)
        return len(self.sems) - 1

    def _chan(self, name):
        if name not in self.ch:
            self.ch[name] = [self._new_sem(), 0]
        return self.ch[name]

    def bg(self, name):
        self._chan(name)
        self.bgset.add(name)

    def point(self, ch="main"):
        si, cnt = self._chan(ch)
        return (si, cnt)

    def all_points(self):
        return [(si, cnt) for nm, (si, cnt) in self.ch.items()
                if nm not in self.bgset]

    def final_points(self):
        return [(si, cnt) for si, cnt in self.ch.values()]

    def step(self, eng, fns, dma=False, ch="main", waits="all"):
        if not isinstance(fns, (list, tuple)):
            fns = [fns]
        c = self._chan(ch)
        if c[1] >= SEM_ROT:
            old_si, old_cnt = c
            c[0] = self._new_sem()
            c[1] = 0
            self.steps.append(("BARRIER", old_si, old_cnt, None, None))
        if waits == "all":
            w = self.all_points()
        else:
            w = [p for p in waits if p is not None]
        if eng in ("vector", "scalar", "gpsimd") and (eng, ch) in self.last_eng:
            w = w + [self.last_eng[(eng, ch)]]
        q = 16 if dma else 1
        self.steps.append((eng, c[0], list(w), list(fns), q))
        c[1] += q * len(fns) if dma else 1
        self.last_eng[(eng, ch)] = (c[0], c[1])
        return (c[0], c[1])

    def dma(self, fn, ch="main", waits="all", eng="sync"):
        return self.step(eng, [fn], dma=True, ch=ch, waits=waits)

    def check(self):
        engines = {}
        for i, st in enumerate(self.steps):
            if st[0] == "BARRIER":
                for e in ("sync", "vector", "scalar", "tensor", "gpsimd"):
                    engines.setdefault(e, []).append(("B", st[1], st[2], i))
            else:
                engines.setdefault(st[0], []).append(("S", st, i))
        counts = {}
        heads = {e: 0 for e in engines}
        progress = True
        while progress:
            progress = False
            for e, q in engines.items():
                while heads[e] < len(q):
                    item = q[heads[e]]
                    if item[0] == "B":
                        if counts.get(item[1], 0) < item[2]:
                            break
                    else:
                        _, si, w, fns, qq = item[1]
                        if any(counts.get(ws, 0) < wv for ws, wv in w):
                            break
                        counts[si] = counts.get(si, 0) + (qq * len(fns) if qq == 16 else 1)
                    heads[e] += 1
                    progress = True
        stuck = {e: (len(q) - heads[e]) for e, q in engines.items() if heads[e] < len(q)}
        if stuck:
            detail = []
            for e, q in engines.items():
                if heads[e] < len(q):
                    item = q[heads[e]]
                    if item[0] == "B":
                        detail.append(f"{e}@step{item[3]}: BARRIER sem{item[1]} needs {item[2]} has {counts.get(item[1], 0)}")
                    else:
                        _, si, w, fns, qq = item[1]
                        missing = [(ws, wv, counts.get(ws, 0)) for ws, wv in w if counts.get(ws, 0) < wv]
                        detail.append(f"{e}@step{item[3]}: needs {missing}")
            raise AssertionError("DEADLOCK: " + "; ".join(detail[:5]) + f" stuck={stuck}")

    def emit(self):
        self.check()
        used = set(st[0] for st in self.steps if st[0] != "BARRIER")
        used.add("sync")
        nc, sems, steps = self.nc, self.sems, self.steps
        finals = self.final_points()

        def emit_for(e, name):
            seen = {}
            for st in steps:
                if st[0] == "BARRIER":
                    if seen.get(st[1], -1) < st[2]:
                        e.wait_ge(sems[st[1]], st[2])
                        seen[st[1]] = st[2]
                    continue
                if st[0] != name:
                    continue
                _, si, w, fns, q = st
                for wsi, wval in w:
                    if wval > 0 and seen.get(wsi, -1) < wval:
                        e.wait_ge(sems[wsi], wval)
                        seen[wsi] = wval
                for i, fn in enumerate(fns):
                    inst = fn(e)
                    if q == 16 or i == len(fns) - 1:
                        inst.then_inc(sems[si], q)
            if name == "sync":
                for wsi, wval in finals:
                    if wval > 0:
                        e.wait_ge(sems[wsi], wval)

        with nc.Block() as block:
            for eng_name in ("sync", "vector", "scalar", "tensor", "gpsimd"):
                if eng_name not in used:
                    continue
                def mk(nm):
                    def f(e):
                        emit_for(e, nm)
                    return f
                getattr(block, eng_name)(mk(eng_name))


class Arena:

    def __init__(self, t, kib):
        try:
            self.base = t[:]
        except TypeError:
            self.base = t
        if self.base.ndim == 3:
            self.base = self.base.rearrange("p a b -> p (a b)")
        elif self.base.ndim == 4:
            self.base = self.base.rearrange("p a b c -> p (a b c)")
        self.t = t
        self.cap = kib * 1024
        self.off = 0

    def reset(self, byte_off=0):
        self.off = byte_off

    def take(self, shape, dt):
        nel = 1
        for s in shape[1:]:
            nel *= s
        esz = {F32: 4, BF16: 2, I32: 4, I8: 1}[dt]
        start = (self.off + 31) & ~31
        self.off = start + nel * esz
        assert self.off <= self.cap, f"arena overflow {self.off} > {self.cap}"
        e0 = start // esz
        v = self.base.bitcast(dt)[0:shape[0], e0:e0 + nel]
        if len(shape) > 2:
            pat = "p (" + " ".join(f"d{i}" for i in range(1, len(shape))) + ") -> p " + \
                  " ".join(f"d{i}" for i in range(1, len(shape)))
            kw = {f"d{i}": shape[i] for i in range(2, len(shape))}
            v = v.rearrange(pat, **kw)
        return v


def _ln_rows(S, row, p_sum, p_sq, p_ba, p_bb, abc, bbc, ones_f, eps_ap):
    m, s2, t1, a1 = row[0:1, 0:512], row[0:1, 512:1024], row[0:1, 1024:1536], row[0:1, 1536:2048]
    S.step("scalar", lambda e: e.mul(m, p_sum, 1.0 / 1024.0))
    S.step("scalar", lambda e: e.mul(s2, p_sq, 1.0 / 1024.0))
    S.step("scalar", lambda e: e.activation(t1, m, AF.Square))
    S.step("vector", lambda e: e.tensor_sub(a1, s2, t1))
    S.step("scalar", lambda e: e.activation(t1, a1, AF.Ln, bias=eps_ap))
    S.step("scalar", lambda e: e.activation(s2, t1, AF.Exp, scale=-0.5))
    S.step("vector", lambda e: e.tensor_mul(t1, m, s2))
    S.step("vector", lambda e: e.tensor_scalar_mul(m, t1, -1.0))
    S.step("tensor", lambda e: e.matmul(p_ba, ones_f[0:1, 0:128], s2,
                                        start=True, stop=True))
    S.step("tensor", lambda e: e.matmul(p_bb, ones_f[0:1, 0:128], m,
                                        start=True, stop=True))
    S.step("scalar", lambda e: e.copy(abc[:], p_ba))
    S.step("scalar", lambda e: e.copy(bbc[:], p_bb))


def _ln_rows_p(S, w, row, p_sum, p_sq, p_ba, p_bb, abc, bbc, ones_f, eps_ap,
               first, prev_bc_readers):
    m, s2, t1, a1 = row[0:1, 0:512], row[0:1, 512:1024], row[0:1, 1024:1536], row[0:1, 1536:2048]
    pa1 = S.step("scalar", lambda e: e.mul(m, p_sum, 1.0 / 1024.0),
                 ch="act", waits=w("act", first))
    S.step("scalar", lambda e: e.mul(s2, p_sq, 1.0 / 1024.0), ch="act", waits=[])
    pa3 = S.step("scalar", lambda e: e.activation(t1, m, AF.Square), ch="act", waits=[])
    pd1 = S.step("vector", lambda e: e.tensor_sub(a1, s2, t1), ch="dve",
                 waits=w("dve", [pa3]))
    pa4 = S.step("scalar", lambda e: e.activation(t1, a1, AF.Ln, bias=eps_ap),
                 ch="act", waits=[pd1])
    pa5 = S.step("scalar", lambda e: e.activation(s2, t1, AF.Exp, scale=-0.5),
                 ch="act", waits=[])
    pd2 = S.step("vector", lambda e: e.tensor_mul(t1, m, s2), ch="dve", waits=[pa5])
    pd3 = S.step("vector", lambda e: e.tensor_scalar_mul(m, t1, -1.0), ch="dve", waits=[])
    pm1 = S.step("tensor", lambda e: e.matmul(p_ba, ones_f[0:1, 0:128], s2,
                                              start=True, stop=True),
                 ch="pe", waits=w("pe", [pa5] + prev_bc_readers))
    pm2 = S.step("tensor", lambda e: e.matmul(p_bb, ones_f[0:1, 0:128], m,
                                              start=True, stop=True),
                 ch="pe", waits=[pd3])
    pab = S.step("scalar", lambda e: e.copy(abc[:], p_ba), ch="act", waits=[pm1])
    pbb = S.step("scalar", lambda e: e.copy(bbc[:], p_bb), ch="act", waits=[pm2])
    return pab, pbb, pm2



def emit_l0(nc, es, S, tap, A, P0, P1, P2, P3, xn, qf, asc, wbig, wo_sb, kv_sb, u_sb, ones_bf, ones_f, fc_sb, x, wqkvT, wguT, wdT, woT, cosq, sinq, cosk, sink, ind_den, ind3, fconst, y):
    def wv(kc, lo, hi):
        return wbig[:, kc * 3072 + lo: kc * 3072 + hi]

    def wdv(jc, lo, hi):
        return wbig[:, jc * 1024 + lo: jc * 1024 + hi]

    def kvslot(h, eo):
        a = 32 * (h % 4)
        b = 2 * (h // 4) + eo
        col = 65 * b if b < 7 else 512
        return a, col

    S.step("vector", lambda e: e.memset(ones_bf[:], 1.0))
    S.step("vector", lambda e: e.memset(ones_f[:], 1.0))
    S.dma(lambda e: e.dma_start(out=fc_sb[:], in_=fconst[:]))

    S.bg("pdi0")
    S.bg("pdi1")
    S.bg("pcast")
    Aq = Arena(asc, 32)
    pst_f = [Aq.take([128, 1024], F32) for _ in range(2)]
    prep_state = {"i": 0, "pi": {}, "pc": {}, "done_qkv": None}
    all_jobs = [("q", i // 3, i % 3) for i in range(24)]

    def _prep_in(i):
        _, c, j = all_jobs[i]
        pb = i % 2
        st = prep_state
        st["pi"][i] = S.dma(lambda e, c=c, j=j, pb=pb: e.dma_start(
            out=pst_f[pb],
            in_=wqkvT[c * 128:(c + 1) * 128, j * 1024:(j + 1) * 1024]),
            ch=f"pdi{pb}", waits=[st["pc"].get(i - 2)])

    def _prep_fin(i):
        _, c, j = all_jobs[i]
        pb = i % 2
        st = prep_state
        st["pc"][i] = S.step("vector", lambda e, c=c, j=j, pb=pb: e.tensor_copy(
            wv(c, j * 1024, (j + 1) * 1024), pst_f[pb]),
            ch="pcast", waits=[st["pi"][i]])
        if i == len(all_jobs) - 1:
            st["done_qkv"] = st["pc"][i]

    def pump(n=1):
        st = prep_state
        for _ in range(n):
            site = st["i"]
            if site >= len(all_jobs) + 2:
                return
            if site - 2 >= 0:
                _prep_fin(site - 2)
            if site < len(all_jobs):
                _prep_in(site)
            st["i"] = site + 1

    A.reset()
    xst = A.take([128, 2, TTS], F32)
    xbf = A.take([128, 8, TTS], BF16)
    sqs = A.take([128, 2, TTS], BF16)
    row = A.take([1, 2048], F32)
    abc = A.take([128, TTS], F32)
    bbc = A.take([128, TTS], F32)
    tmul = A.take([128, 2, TTS], BF16)
    entry = S.all_points()
    started = set()

    def w(ch, pts):
        pts = [p for p in pts if p is not None]
        if ch not in started:
            started.add(ch)
            pts = pts + entry
        return pts

    xre = x.rearrange("(c p) t -> p c t", p=128)
    p_cp = {}
    p_sq = {}
    p_ms = p_mq = p_mu = None
    p_ap = {}
    pab = pbb = None
    for tt in range(TTN):
        tsl = slice(tt * TTS, (tt + 1) * TTS)
        for c in range(8):
            pb = c % 2
            p_x = S.dma(lambda e, c=c, tsl=tsl, pb=pb: e.dma_start(
                out=xst[:, pb, :], in_=xre[:, c, tsl]),
                ch=f"xdma{pb}", waits=w(f"xdma{pb}",
                                        [p_cp.get(c - 2, p_cp.get(c + 6)),
                                         p_sq.get(c - 2, p_sq.get(c + 6))]))
            p_cp[c] = S.step("vector", lambda e, c=c, pb=pb: e.tensor_copy(
                xbf[:, c, :], xst[:, pb, :]), ch="dve", waits=w("dve", [p_x]))
            p_sq[c] = S.step("scalar", lambda e, pb=pb: e.activation(
                sqs[:, pb, :], xst[:, pb, :], AF.Square),
                ch="act", waits=w("act", [p_x, p_mq]))
            p_ms = S.step("tensor", lambda e, c=c: e.matmul(
                P1[0:1, 0:512], ones_bf[:, 0:1], xbf[:, c, :],
                start=(c == 0), stop=(c == 7)),
                ch="pe", waits=w("pe", [p_cp[c]] + ([pab] if c == 0 else [])))
            p_mq = S.step("tensor", lambda e, c=c, pb=pb: e.matmul(
                P1[0:1, 512:1024], ones_bf[:, 0:1], sqs[:, pb, :],
                start=(c == 0), stop=(c == 7)),
                ch="pe", waits=[p_sq[c]])
            if c % 2:
                pump(1)
        pab, pbb, _ = _ln_rows_p(S, w, row, P1[0:1, 0:512], P1[0:1, 512:1024],
                                 P0[:, 0:512], P0[:, 512:1024], abc, bbc, ones_f,
                                 fc_sb[0:1, 0:1], [p_ms, p_mq],
                                 [pab, pbb])
        for c in range(8):
            pb = c % 2
            p_mu = S.step("vector", lambda e, c=c, pb=pb: e.tensor_mul(
                tmul[:, pb, :], xbf[:, c, :], abc[:]), ch="dve",
                waits=[pab])
            p_ap[c] = S.step("vector", lambda e, c=c, tsl=tsl, pb=pb: e.tensor_add(
                xn[:, c, tsl], tmul[:, pb, :], bbc[:]), ch="dve", waits=[pbb])
            if c % 2:
                pump(1)
    t_ = tap("xn", [D, T], BF16)
    if t_ is not None:
        S.dma(lambda e, t_=t_: e.dma_start(
            out=t_.rearrange("(c p) t -> p c t", p=128), in_=xn[:]))

    A.reset()
    cqs = A.take([128, T], BF16)
    sqt = A.take([128, T], BF16)
    ckb = A.take([128, 512], BF16)
    skb = A.take([128, 512], BF16)
    qstg = A.take([128, 8, TTS], BF16)
    qrp = A.take([128, 8, TTS], BF16)
    kst = A.take([128, 1024], BF16)
    krp = A.take([128, 1024], BF16)
    kf_tb = A.take([128, 1024], BF16)
    kesc1 = A.take([128, 1024], BF16)
    kesc2 = A.take([128, 1024], BF16)
    vaug = A.take([128, 1040], BF16)
    entry = S.all_points()
    started = set()

    def w(ch, pts):
        pts = [p for p in pts if p is not None]
        if ch not in started:
            started.add(ch)
            pts = pts + entry
        return pts

    while prep_state["done_qkv"] is None:
        pump(1)
    entry = entry + [prep_state["done_qkv"]]
    S.dma(lambda e: e.dma_start(out=cqs, in_=cosq[:]), ch="wdma",
          waits=w("wdma", []))
    p_tab = S.dma(lambda e: e.dma_start(out=sqt, in_=sinq[:]), ch="wdma", waits=[])
    p_vg = S.step("vector", lambda e: e.memset(vaug, 1.0), ch="dve",
                  waits=w("dve", []))

    p_evq = {}
    p_rope_last = None
    p_kv = None
    p_kelu = None
    p_vc = {}
    p_evk = {}
    p_ck = None
    for tt in range(TTN):
        tsl = slice(tt * TTS, (tt + 1) * TTS)
        for oc in range(8):
            qb = (oc % 2) * 512
            p_mm = S.step("tensor", [
                (lambda e, kc=kc, oc=oc, tsl=tsl, qb=qb: e.matmul(
                    P0[:, qb:qb + 512], wv(kc, oc * 128, oc * 128 + 128),
                    xn[:, kc, tsl],
                    start=(kc == 0), stop=(kc == 7))) for kc in range(8)],
                ch="pe", waits=w("pe", [p_evq.get(oc - 2, p_evq.get(oc + 6))]))
            p_evq[oc] = S.step("scalar", lambda e, oc=oc, qb=qb: e.copy(
                qstg[:, oc, :], P0[:, qb:qb + 512]),
                ch="act", waits=w("act", [p_mm, p_rope_last, p_kelu]))
            pump(1)
        t1q, t2q = kesc1[:, 0:512], kesc1[:, 512:1024]
        for c in range(4):
            S.step("vector", [
                (lambda e, c=c, tsl=tsl, t1q=t1q: e.tensor_mul(
                    t1q, qstg[:, c, :], cqs[:, tsl])),
                (lambda e, c=c, tsl=tsl, t2q=t2q: e.tensor_mul(
                    t2q, qstg[:, c + 4, :], sqt[:, tsl]))],
                ch="dve", waits=[p_evq[c], p_evq[c + 4], p_kv, p_tab])
            S.step("vector", lambda e, c=c, t1q=t1q, t2q=t2q: e.tensor_sub(
                qrp[:, c, :], t1q, t2q), ch="dve", waits=[])
            S.step("vector", [
                (lambda e, c=c, tsl=tsl, t1q=t1q: e.tensor_mul(
                    t1q, qstg[:, c, :], sqt[:, tsl])),
                (lambda e, c=c, tsl=tsl, t2q=t2q: e.tensor_mul(
                    t2q, qstg[:, c + 4, :], cqs[:, tsl]))],
                ch="dve", waits=[])
            p_rope_last = S.step("vector", lambda e, c=c, t1q=t1q, t2q=t2q:
                                 e.tensor_add(qrp[:, c + 4, :], t1q, t2q),
                                 ch="dve", waits=[])
        k2v = kesc2.rearrange("p (a b) -> p a b", b=512)
        kst3 = kst.rearrange("p (a b) -> p a b", b=512)
        for g in range(4):
            gs = slice(2 * g, 2 * g + 2)
            p_min = S.step("vector", lambda e, gs=gs, k2v=k2v: e.tensor_scalar_min(
                k2v, qrp[:, gs, :], 0.0), ch="dve", waits=[])
            p_exp = S.step("scalar", lambda e, k2v=k2v, kst3=kst3: e.activation(
                kst3, k2v, AF.Exp), ch="act", waits=[p_min])
            p_max = S.step("vector", lambda e, gs=gs, k2v=k2v: e.tensor_scalar_max(
                k2v, qrp[:, gs, :], 0.0), ch="dve", waits=[p_exp])
            p_rope_last = S.step("vector", lambda e, gs=gs, tsl=tsl, k2v=k2v,
                                 kst3=kst3: e.tensor_add(
                qf[:, gs, tsl], k2v, kst3), ch="dve", waits=[])
        for tb in range(4):
            tbg = tt * 4 + tb
            tbs = slice(tt * TTS + tb * 128, tt * TTS + tb * 128 + 128)
            S.dma(lambda e, tbg=tbg: e.dma_start(
                out=ckb, in_=cosk[tbg * 128:(tbg + 1) * 128, :]),
                ch="wdma", waits=[p_kelu])
            p_ck = S.dma(lambda e, tbg=tbg: e.dma_start(
                out=skb, in_=sink[tbg * 128:(tbg + 1) * 128, :]),
                ch="wdma", waits=[])
            for fc in range(2):
                kb = fc * 512
                p_mk = S.step("tensor", [
                    (lambda e, kc=kc, fc=fc, tbs=tbs, kb=kb: e.matmul(
                        P1[:, kb:kb + 512], xn[:, kc, tbs],
                        wv(kc, 1024 + fc * 512, 1536 + fc * 512),
                        start=(kc == 0), stop=(kc == 7))) for kc in range(8)],
                    ch="pe", waits=[p_evk.get((tb - 1, fc))])
                p_evk[(tb, fc)] = S.step("scalar", lambda e, fc=fc, kb=kb: e.copy(
                    kst[:, fc * 512:fc * 512 + 512], P1[:, kb:kb + 512]),
                    ch="act", waits=[p_mk, p_rope_last, p_kelu])
            p_r1 = S.step("vector", [
                lambda e: e.tensor_mul(krp[:, 0:512], kst[:, 0:512], ckb),
                lambda e: e.tensor_mul(kesc1[:, 0:512], kst[:, 512:1024], skb)],
                ch="dve", waits=[p_evk[(tb, 0)], p_evk[(tb, 1)], p_ck, p_kv])
            S.step("vector", lambda e: e.tensor_sub(krp[:, 0:512], krp[:, 0:512],
                                                    kesc1[:, 0:512]),
                   ch="dve", waits=[])
            S.step("vector", [
                lambda e: e.tensor_mul(krp[:, 512:1024], kst[:, 0:512], skb),
                lambda e: e.tensor_mul(kesc1[:, 512:1024], kst[:, 512:1024], ckb)],
                ch="dve", waits=[])
            S.step("vector", lambda e: e.tensor_add(krp[:, 512:1024],
                                                    krp[:, 512:1024],
                                                    kesc1[:, 512:1024]),
                   ch="dve", waits=[])
            qs2 = qstg[:, 0:2, :].rearrange("p a b -> p (a b)")
            p_min = S.step("vector", lambda e: e.tensor_scalar_min(
                kesc2, krp, 0.0), ch="dve", waits=[])
            p_exp = S.step("scalar", lambda e, qs2=qs2: e.activation(
                qs2, kesc2, AF.Exp), ch="act", waits=[p_min])
            p_max = S.step("vector", lambda e: e.tensor_scalar_max(
                kesc2, krp, 0.0), ch="dve", waits=[p_exp])
            p_kelu = S.step("vector", lambda e, qs2=qs2: e.tensor_add(
                kf_tb, kesc2, qs2), ch="dve", waits=[])
            for fc in range(2):
                vb = fc * 512
                p_mv = S.step("tensor", [
                    (lambda e, kc=kc, fc=fc, tbs=tbs, vb=vb: e.matmul(
                        P3[:, vb:vb + 512], xn[:, kc, tbs],
                        wv(kc, 2048 + fc * 512, 2560 + fc * 512),
                        start=(kc == 0), stop=(kc == 7))) for kc in range(8)],
                    ch="pe", waits=[p_vc.get((tb - 1, fc)), p_kv])
                p_vc[(tb, fc)] = S.step("vector", lambda e, fc=fc, vb=vb: e.tensor_copy(
                    vaug.rearrange("p (h v) -> p h v", v=65)[:, fc * 8:fc * 8 + 8, 0:64],
                    P3[:, vb:vb + 512].rearrange("p (h v) -> p h v", v=64)),
                    ch="dve", waits=[p_mv, p_vg, p_kv])
            mms = []
            order = []
            for h in range(NH):
                for eo in range(2):
                    a, b = kvslot(h, eo)
                    lead = (b == 0 and eo == 0) or (h // 4 == 3 and eo == 1)
                    order.append((0 if lead else 1, h, eo, a, b, lead))
            order.sort()
            for _, h, eo, a, b, lead in order:
                col = 32 * h + eo * 512
                mms.append(lambda e, a=a, b=b, col=col, h=h, tbg=tbg, lead=lead:
                           e.matmul(
                    P2[a:a + 32, b:b + 65], kf_tb[:, col:col + 32],
                    vaug[:, 65 * h:65 * h + 65],
                    start=(tbg == 0 and lead), stop=(tbg == 15),
                    tile_position=(0, a)))
            p_kv = S.step("tensor", mms, ch="pe",
                          waits=[p_kelu, p_vc[(tb, 0)], p_vc[(tb, 1)]])
            pump(1)
    t_ = tap("qf", [D, T], BF16)
    if t_ is not None:
        S.dma(lambda e, t_=t_: e.dma_start(
            out=t_.rearrange("(c p) t -> p c t", p=128), in_=qf[:]))
    p_b_end = S.point("pe")
    S.step("scalar", lambda e: e.copy(kv_sb[:], P2[:]))
    for h in range(NH):
        for eo in range(2):
            a, b = kvslot(h, eo)
            pc = (h // 4) + 4 * eo
            S.step("vector", lambda e, a=a, b=b, pc=pc: e.tensor_copy(
                u_sb[a:a + 32, pc:pc + 1], kv_sb[a:a + 32, b + 64:b + 65]))
    t_ = tap("kv", [128, 1024])
    if t_ is not None:
        A2 = Arena(A.t, A.cap // 1024)
        A2.reset(38 * 1024)
        kvf = A2.take([128, 1024], F32)
        S.step("vector", lambda e, kvf=kvf: e.tensor_copy(kvf, kv_sb[:]))
        S.dma(lambda e, t_=t_, kvf=kvf: e.dma_start(out=t_[:], in_=kvf))

    A.reset()
    prod = A.take([128, 8, TTS], BF16)
    indd = A.take([128, 8, 16], BF16)
    ind3s = A.take([16, 8, 128], BF16)
    dens = A.take([16, TTS], F32)
    rs = A.take([16, TTS], F32)
    rbf = A.take([16, TTS], BF16)
    rbc = A.take([128, 2, TTS], F32)
    while prep_state["i"] < len(all_jobs) + 2:
        pump(1)
    p_prep_qkv = prep_state["done_qkv"]
    t_ = tap("wb", [128, 24576], BF16)
    if t_ is not None:
        S.bg("tdbg")
        S.dma(lambda e, t_=t_: e.dma_start(out=t_[:], in_=wbig[:]), ch="tdbg",
              waits=[p_prep_qkv])
    entry = S.all_points() + [p_prep_qkv]
    started = set()

    def w(ch, pts):
        pts = [p for p in pts if p is not None]
        if ch not in started:
            started.add(ch)
            pts = pts + entry
        return pts

    S.dma(lambda e: e.dma_start(out=indd, in_=ind_den.rearrange("c p h -> p c h")),
          ch="wdma", waits=w("wdma", []))
    p_ind = S.dma(lambda e: e.dma_start(out=ind3s, in_=ind3.rearrange("c h p -> h c p")),
                  ch="wdma", waits=[])
    p_dm = p_max = p_rbf_prev = p_rbc_last = None
    p_ev = {}
    p_mult = {}
    for tt in range(TTN):
        tsl = slice(tt * TTS, (tt + 1) * TTS)
        p_pr = S.step("vector", [
            (lambda e, c=c, tsl=tsl: e.tensor_scalar_mul(
                prod[:, c, :], qf[:, c, tsl], u_sb[:, c:c + 1])) for c in range(8)],
            ch="dve", waits=w("dve", [p_dm]))
        p_dm = S.step("tensor", [
            (lambda e, c=c: e.matmul(P1[0:16, 0:512], indd[:, c, :], prod[:, c, :],
                                     start=(c == 0), stop=(c == 7))) for c in range(8)],
            ch="pe", waits=w("pe", [p_pr, p_max, p_ind]))
        p_max = S.step("vector", lambda e: e.tensor_scalar_max(
            dens, P1[0:16, 0:512], 1e-6), ch="dve", waits=[p_dm])
        p_ln = S.step("scalar", lambda e: e.activation(dens, dens, AF.Ln),
                      ch="act", waits=w("act", [p_max, p_rbf_prev]))
        p_rs = S.step("scalar", lambda e: e.activation(
            rs, dens, AF.Exp, scale=-1.0, bias=fc_sb[0:16, 2:3]),
            ch="act", waits=[p_ln])
        p_rbf = S.step("vector", lambda e: e.tensor_copy(rbf, rs),
                       ch="dve", waits=[p_rs, p_rbc_last])
        p_rbf_prev = p_rbf
        for c in range(8):
            pb = c % 2
            ab = pb * 512
            mms = []
            for j, h in enumerate((2 * c, 2 * c + 1)):
                for eo in range(2):
                    a, b = kvslot(h, eo)
                    pc = (h // 4) + 4 * eo
                    mms.append(lambda e, a=a, b=b, pc=pc, j=j, eo=eo, tsl=tsl, ab=ab:
                               e.matmul(P0[64 * j:64 * j + 64, ab:ab + 512],
                                        kv_sb[a:a + 32, b:b + 64],
                                        qf[a:a + 32, pc, tsl],
                                        start=(eo == 0), stop=(eo == 1),
                                        tile_position=(a, 64 * j)))
            p_att = S.step("tensor", mms, ch="pe", waits=[p_mult.get(c - 2)])
            p_rbc = S.step("tensor", lambda e, c=c, pb=pb: e.matmul(
                P3[:, pb * 512:pb * 512 + 512], ind3s[:, c, :], rbf,
                start=True, stop=True),
                ch="pe", waits=[p_rbf, p_ev.get(c - 2)])
            p_rbc_last = p_rbc
            p_ev[c] = S.step("scalar", lambda e, pb=pb: e.copy(
                rbc[:, pb, :], P3[:, pb * 512:pb * 512 + 512]),
                ch="act", waits=[p_rbc])
            p_mult[c] = S.step("vector", lambda e, c=c, tsl=tsl, pb=pb, ab=ab:
                               e.tensor_mul(asc[:, c, tsl], P0[:, ab:ab + 512],
                                            rbc[:, pb, :]),
                               ch="dve", waits=[p_att, p_ev[c]])
        p_mult = {-2: p_mult[6], -1: p_mult[7]}
        p_ev = {-2: p_ev[6], -1: p_ev[7]}
    t_ = tap("asc", [D, T], BF16)
    if t_ is not None:
        S.dma(lambda e, t_=t_: e.dma_start(
            out=t_.rearrange("(c p) t -> p c t", p=128), in_=asc[:]))

    Ap2 = Arena(ar_t_for_prep2 := A.t, 40)
    Ap2.reset(22528)
    p2_i = [Ap2.take([128, 1024], I8) for _ in range(2)]
    p2_f = [Ap2.take([128, 1024], F32) for _ in range(2)]
    p2c = {}
    p2i = {}
    for i in range(32):
        pb = i % 2
        if i < 24:
            jc = i
            p2i[i] = S.dma(lambda e, jc=jc, pb=pb: e.dma_start(
                out=p2_i[pb], in_=wdT[jc * 128:(jc + 1) * 128, :]),
                ch=f"pdi{pb}", waits=[p2c.get(i - 2), p_prep_qkv])
            p2c[i] = S.step("gpsimd", lambda e, jc=jc, pb=pb: e.tensor_copy(
                wdv(jc, 0, 1024), p2_i[pb]),
                ch="pcast", waits=[p2i[i], p_b_end])
        else:
            c = i - 24
            p2i[i] = S.dma(lambda e, c=c, pb=pb: e.dma_start(
                out=p2_f[pb], in_=woT[c * 128:(c + 1) * 128, :]),
                ch=f"pdi{pb}", waits=[p2c.get(i - 2)])
            p2c[i] = S.step("gpsimd", lambda e, c=c, pb=pb: e.tensor_copy(
                wo_sb[:, c, :], p2_f[pb]),
                ch="pcast", waits=[p2i[i]])
    p_prep2 = p2c[31]

    def _dstep(eng, fns, ch="main", waits="all"):
        return S.step(eng, fns, ch=ch, waits=waits)

    def _ddma(fn, ch="main", waits="all", eng="sync"):
        return S.dma(fn, ch=ch, waits=waits, eng=eng)

    A.reset()
    htile = A.take([128, 24, TTS], BF16)
    wgu = A.take([128, 2, 2048], BF16)
    wgu8 = Arena(qf, 32).take([128, 4, 2048], I8)
    mark = A.off
    sg = A.take([128, 2, TTS], F32)
    A.reset(mark)
    ts1 = A.take([128, TTS], F32)
    ts2 = A.take([128, TTS], F32)
    ts3 = A.take([128, TTS], F32)
    entry = S.all_points() + [p_prep2]
    started = set()

    def w(ch, pts):
        pts = [p for p in pts if p is not None]
        if ch not in started:
            started.add(ch)
            pts = pts + entry
        return pts

    p_pe = {}
    p_cast = {}
    p_dma = {}
    p_act_by = {0: None, 1: None}
    p_dve_by = {0: None, 1: None}
    p_oact = {}
    p_st = None
    NS = TTN * 24

    def _gu_dma(s):
        jc2 = s % 24
        q = s % 4
        p_dma[s] = _ddma(lambda e, jc=jc2, q=q: e.dma_start(
            out=wgu8[:, q, :], in_=wguT[jc]),
            ch=f"wdma{q}", eng="sync",
            waits=w(f"wdma{q}", [p_cast.get(s - 4)]))

    def _gu_cast(s):
        pb = s % 2
        q = s % 4
        p_cast[s] = _dstep("vector", lambda e, pb=pb, q=q: e.tensor_copy(
            wgu[:, pb, :], wgu8[:, q, :]), ch="dve",
            waits=w("dve", [p_dma[s], p_pe.get(s - 2)]))

    for tt in range(TTN):
        tsl = slice(tt * TTS, (tt + 1) * TTS)
        for jc in range(24):
            s = tt * 24 + jc
            pb = s % 2
            pg = P0 if pb == 0 else P2
            if s == 0:
                for k in range(3):
                    _gu_dma(k)
                _gu_cast(0)
            mmg = [(lambda e, kc=kc, tsl=tsl, pb=pb, pg=pg: e.matmul(
                pg[:, 0:512], wgu[:, pb, kc * 128:(kc + 1) * 128], xn[:, kc, tsl],
                start=(kc == 0), stop=(kc == 7))) for kc in range(8)]
            mmu = [(lambda e, kc=kc, tsl=tsl, pb=pb, pg=pg: e.matmul(
                pg[:, 512:1024], wgu[:, pb, 1024 + kc * 128:1024 + (kc + 1) * 128],
                xn[:, kc, tsl],
                start=(kc == 0), stop=(kc == 7))) for kc in range(8)]
            p_pe[s] = _dstep("tensor", mmg + mmu, ch="pe",
                             waits=w("pe", [p_cast[s], p_act_by[pb], p_dve_by[pb]]))
            if s + 3 < NS:
                _gu_dma(s + 3)
            if s + 1 < NS:
                _gu_cast(s + 1)
            p_act_by[pb] = _dstep("scalar", lambda e, pb=pb, pg=pg: e.activation(
                sg[:, pb, :], pg[:, 0:512], AF.Silu, scale=DQ),
                ch="act", waits=w("act", [p_pe[s]]))
            p_dve_by[pb] = _dstep("vector", lambda e, jc=jc, pb=pb, pg=pg:
                                  e.tensor_mul(
                htile[:, jc, :], sg[:, pb, :], pg[:, 512:1024]),
                ch="dve", waits=w("dve", [p_act_by[pb]]))
        p_dve = p_dve_by[1]
        p_act = p_act_by[1]
        for oc in range(8):
            pq = P1 if oc % 2 == 0 else P3
            mms = [
                (lambda e, jc=jc, oc=oc, pq=pq: e.matmul(
                    pq[:, 0:512], wdv(jc, oc * 128, oc * 128 + 128), htile[:, jc, :],
                    start=(jc == 0), stop=False)) for jc in range(24)]
            mms += [
                (lambda e, vc=vc, oc=oc, tsl=tsl, pq=pq: e.matmul(
                    pq[:, 0:512], wo_sb[:, vc, oc * 128:(oc + 1) * 128], asc[:, vc, tsl],
                    start=False, stop=(vc == 7))) for vc in range(8)]
            p_mm = _dstep("tensor", mms, ch="pe", waits=w("pe", [
                p_oact.get(oc % 2), p_dve if oc == 0 else None]))
            p_x = _ddma(lambda e, oc=oc, tsl=tsl: e.dma_start(
                out=ts2, in_=x.rearrange("(c p) t -> p c t", p=128)[:, oc, tsl]),
                ch="xdma", waits=w("xdma", [p_dve]))
            p_act = _dstep("scalar", lambda e, pq=pq: e.mul(ts1, pq[:, 0:512],
                                                           DQ * DQ),
                           ch="act", waits=w("act", [p_mm, p_dve]))
            p_oact[oc % 2] = p_act
            p_dve = _dstep("vector", lambda e: e.tensor_add(ts3, ts1, ts2),
                           ch="dve", waits=w("dve", [p_act, p_x, p_st]))
            p_st = _ddma(lambda e, oc=oc, tsl=tsl: e.dma_start(
                out=y.rearrange("(c p) t -> p c t", p=128)[:, oc, tsl], in_=ts3),
                ch="st", waits=w("st", [p_dve]))


def build_l0(taps=()):
    nc = bass.Bass("TRN2", target_bir_lowering=False)
    es = ExitStack()
    S = Sched(nc, es)

    x = nc.dram_tensor("x", [D, T], F32, kind="ExternalInput")
    wqkvT = nc.dram_tensor("wqkvT", [D, HID], F32, kind="ExternalInput")
    wguT = nc.dram_tensor("wguT", [24, 128, 2048], I8, kind="ExternalInput")
    wdT = nc.dram_tensor("wdT", [HID, D], I8, kind="ExternalInput")
    woT = nc.dram_tensor("woT", [D, D], F32, kind="ExternalInput")
    cosq = nc.dram_tensor("cosq", [128, T], BF16, kind="ExternalInput")
    sinq = nc.dram_tensor("sinq", [128, T], BF16, kind="ExternalInput")
    cosk = nc.dram_tensor("cosk", [T, 512], BF16, kind="ExternalInput")
    sink = nc.dram_tensor("sink", [T, 512], BF16, kind="ExternalInput")
    ind_den = nc.dram_tensor("ind_den", [8, 128, 16], BF16, kind="ExternalInput")
    ind3 = nc.dram_tensor("ind3", [8, 16, 128], BF16, kind="ExternalInput")
    fconst = nc.dram_tensor("fconst", [128, 8], F32, kind="ExternalInput")
    y = nc.dram_tensor("y", [D, T], F32, kind="ExternalOutput")

    dbg = {}

    def tap(name, shape, dt=F32):
        if name in taps:
            dbg[name] = nc.dram_tensor("dbg_" + name, shape, dt, kind="ExternalOutput")
        return dbg.get(name)

    def sbt(name, shape, dt):
        return es.enter_context(nc.sbuf_tensor(name, shape, dt))

    xn = sbt("xn", [128, 8, T], BF16)
    qf = sbt("qf", [128, 8, T], BF16)
    asc = sbt("asc", [128, 8, T], BF16)
    wbig = sbt("wbig", [128, 24576], BF16)
    wo_sb = sbt("wo_sb", [128, 8, D], BF16)
    kv_sb = sbt("kv_sb", [128, 1024], BF16)
    u_sb = sbt("u_sb", [128, 8], F32)
    ones_bf = sbt("ones_bf", [128, 128], BF16)
    ones_f = sbt("ones_f", [1, 128], F32)
    fc_sb = sbt("fc_sb", [128, 8], F32)
    ar_t = sbt("arena", [128, 11264], F32)
    A = Arena(ar_t, 44)

    P0 = es.enter_context(nc.psum_tensor("P0", [128, 1024], F32))
    P1 = es.enter_context(nc.psum_tensor("P1", [128, 1024], F32))
    P2 = es.enter_context(nc.psum_tensor("P2", [128, 1024], F32))
    P3 = es.enter_context(nc.psum_tensor("P3", [128, 1024], F32))

    emit_l0(nc, es, S, tap, A, P0, P1, P2, P3, xn, qf, asc, wbig, wo_sb, kv_sb, u_sb, ones_bf, ones_f, fc_sb, x, wqkvT, wguT, wdT, woT, cosq, sinq, cosk, sink, ind_den, ind3, fconst, y)
    S.emit()
    return nc, es


def emit_l1(nc, es, S, tap, A, P0, P1, P2, P3, xn1, q1, k1, vaug1, idb, ones_bf, ones_f, i32, i32T, ga, gb, bqs, bks, bv_bf, bsum, cneg, fc_sb, gam_s, bet_s, x, wqkv1, bqkv1, wo1, bo1, gam, bet, w1T, b1, vr, cT, indH, ind32, ind32T, ident, fconst, y, wqkv1_bf, scr2):
    def avslot(qt):
        if qt < 7:
            return P2[:, qt * 65: qt * 65 + 65]
        if qt < 14:
            return P2[:, 512 + (qt - 7) * 65: 512 + (qt - 7) * 65 + 65]
        return P3[:, (qt - 14) * 65: (qt - 14) * 65 + 65]

    S.dma(lambda e: e.dma_start(out=idb[:], in_=ident[:]))
    S.dma(lambda e: e.dma_start(out=fc_sb[:], in_=fconst[:]))
    S.dma(lambda e: e.dma_start(out=i32[:], in_=ind32[:]))
    S.dma(lambda e: e.dma_start(out=i32T[:], in_=ind32T[:]))
    S.step("vector", lambda e: e.memset(ones_bf[:], 1.0))
    S.step("vector", lambda e: e.memset(ones_f[:], 1.0))
    S.step("vector", lambda e: e.memset(vaug1[:], 1.0))

    A.reset()
    tv1 = A.take([128, 8], F32)
    tv2 = A.take([128, 8], F32)
    rowc = A.take([1, 1024], F32)
    sv_off = A.off
    S.dma(lambda e: e.dma_start(out=gam_s[:], in_=gam.rearrange("(c p) -> p c", p=128),
                                allow_slow_non_contiguous=True))
    S.dma(lambda e: e.dma_start(out=bet_s[:], in_=bet.rearrange("(c p) -> p c", p=128),
                                allow_slow_non_contiguous=True))
    S.dma(lambda e: e.dma_start(out=tv1, in_=bqkv1[0:1024].rearrange("(c p) -> p c", p=128),
                                allow_slow_non_contiguous=True))
    S.step("vector", lambda e: e.tensor_scalar_mul(bqs[:], tv1, SCALE))
    S.dma(lambda e: e.dma_start(out=bks[:], in_=bqkv1[1024:2048].rearrange("(c p) -> p c", p=128),
                                allow_slow_non_contiguous=True))
    S.dma(lambda e: e.dma_start(out=rowc, in_=bqkv1[2048:3072].rearrange("(o f) -> o f", o=1)))
    S.step("vector", lambda e: e.tensor_copy(bv_bf[:], rowc))
    S.dma(lambda e: e.dma_start(out=tv1, in_=b1.rearrange("(c p) -> p c", p=128),
                                allow_slow_non_contiguous=True))
    S.dma(lambda e: e.dma_start(out=tv2, in_=bo1.rearrange("(c p) -> p c", p=128),
                                allow_slow_non_contiguous=True))
    S.step("vector", lambda e: e.tensor_add(bsum[:], tv1, tv2))
    S.bg("pdi0")
    S.bg("pdi1")
    S.bg("pdo0")
    S.bg("pdo1")
    S.bg("pcast")
    l1_entry = S.all_points()
    Aq1 = Arena(q1, 32)
    q_f = [Aq1.take([128, 1024], F32) for _ in range(2)]
    q_b = [Aq1.take([128, 1024], BF16) for _ in range(2)]
    pq_i = {}
    pq_c = {}
    pq_o = {}
    for i in range(24):
        c, j = i // 3, i % 3
        pb = i % 2
        pq_i[i] = S.dma(lambda e, c=c, j=j, pb=pb: e.dma_start(
            out=q_f[pb], in_=wqkv1[c * 128:(c + 1) * 128, j * 1024:(j + 1) * 1024]),
            ch=f"pdi{pb}", waits=[pq_c.get(i - 2)] + (l1_entry if i < 2 else []))
        pq_c[i] = S.step("gpsimd", lambda e, pb=pb: e.tensor_copy(
            q_b[pb], q_f[pb]), ch="pcast", waits=[pq_i[i], pq_o.get(i - 2)])
        pq_o[i] = S.dma(lambda e, c=c, j=j, pb=pb: e.dma_start(
            out=wqkv1_bf[c * 128:(c + 1) * 128, j * 1024:(j + 1) * 1024], in_=q_b[pb]),
            ch=f"pdo{pb}", waits=[pq_c[i]])
    p_prep_q1 = pq_o[23]

    A.reset()
    xst = A.take([128, 2, TTS], F32)
    xbf = A.take([128, 8, TTS], BF16)
    sqs = A.take([128, 2, TTS], BF16)
    row = A.take([1, 2048], F32)
    abc = A.take([128, TTS], F32)
    bbc = A.take([128, TTS], F32)
    tmul = A.take([128, 2, TTS], BF16)
    spart = A.take([128, 8, 4], F32)
    qpart = A.take([128, 8, 4], F32)
    gtmp = A.take([4, 64], F32)
    s1r = A.take([128, 8], F32)
    q1r = A.take([128, 8], F32)
    entry = S.all_points()
    started = set()

    def w(ch, pts):
        pts = [p for p in pts if p is not None]
        if ch not in started:
            started.add(ch)
            pts = pts + entry
        return pts

    xre = x.rearrange("(c p) t -> p c t", p=128)
    p_cp = {}
    p_sq = {}
    p_ms = p_mq = p_mu = p_acc = None
    p_ap = {}
    pab = pbb = None
    for tt in range(TTN):
        tsl = slice(tt * TTS, (tt + 1) * TTS)
        for c in range(8):
            pb = c % 2
            p_x = S.dma(lambda e, c=c, tsl=tsl, pb=pb: e.dma_start(
                out=xst[:, pb, :], in_=xre[:, c, tsl]),
                ch=f"xdma{pb}", waits=w(f"xdma{pb}", [p_cp.get(c - 2), p_sq.get(c - 2)] +
                                        ([p_mu] if c < 2 else [])))
            p_cp[c] = S.step("vector", lambda e, c=c, pb=pb: e.tensor_copy(
                xbf[:, c, :], xst[:, pb, :]), ch="dve", waits=w("dve", [p_x]))
            p_sq[c] = S.step("scalar", lambda e, pb=pb: e.activation(
                sqs[:, pb, :], xst[:, pb, :], AF.Square),
                ch="act", waits=w("act", [p_x, p_mq if c >= 2 else p_acc]))
            p_ms = S.step("tensor", lambda e, c=c: e.matmul(
                P1[0:1, 0:512], ones_bf[:, 0:1], xbf[:, c, :],
                start=(c == 0), stop=(c == 7)),
                ch="pe", waits=w("pe", [p_cp[c]] + ([pab] if c == 0 else [])))
            p_mq = S.step("tensor", lambda e, c=c, pb=pb: e.matmul(
                P1[0:1, 512:1024], ones_bf[:, 0:1], sqs[:, pb, :],
                start=(c == 0), stop=(c == 7)),
                ch="pe", waits=[p_sq[c]])
        pab, pbb, _ = _ln_rows_p(S, w, row, P1[0:1, 0:512], P1[0:1, 512:1024],
                                 P0[:, 0:512], P0[:, 512:1024], abc, bbc, ones_f,
                                 fc_sb[0:1, 0:1], [p_ms, p_mq],
                                 [pab, pbb])
        for c in range(8):
            pb = c % 2
            p_x = S.dma(lambda e, c=c, tsl=tsl, pb=pb: e.dma_start(
                out=xst[:, pb, :], in_=xre[:, c, tsl]),
                ch=f"xdma{pb}", waits=[p_ap.get(c - 2), p_cp[7], p_sq[7]])
            p_mu = S.step("vector", lambda e, pb=pb: e.tensor_mul(
                tmul[:, pb, :], xst[:, pb, :], abc[:]), ch="dve",
                waits=[p_x, pab])
            p_ap[c] = S.step("vector", lambda e, c=c, tsl=tsl, pb=pb: e.tensor_add(
                xn1[:, c, tsl], tmul[:, pb, :], bbc[:]), ch="dve", waits=[pbb])
            S.step("vector", lambda e, c=c, tt=tt, tsl=tsl: e.reduce_sum(
                spart[:, c, tt:tt + 1], xn1[:, c, tsl], axis=AX.X),
                ch="dve", waits=[])
            p_acc = S.step("scalar", lambda e, c=c, tt=tt, tsl=tsl, pb=pb:
                           e.activation(sqs[:, pb, :], xn1[:, c, tsl], AF.Square,
                                        accum_out=qpart[:, c, tt:tt + 1]),
                           ch="act", waits=[p_ap[c], p_mq])
        p_cp = {}
        p_sq = {}
    for c in range(8):
        S.step("vector", lambda e, c=c: e.reduce_sum(s1r[:, c:c + 1], spart[:, c, :], axis=AX.X))
        S.step("vector", lambda e, c=c: e.reduce_sum(q1r[:, c:c + 1], qpart[:, c, :], axis=AX.X))
    S.step("tensor", lambda e: e.matmul(P1[0:4, 0:8], i32[:], s1r, start=True, stop=True))
    S.step("tensor", lambda e: e.matmul(P1[0:4, 8:16], i32[:], q1r, start=True, stop=True))
    NG = 1.0 / (T * 32.0)
    S.step("scalar", lambda e: e.mul(gtmp[0:4, 0:16], P1[0:4, 0:16], NG))
    S.step("scalar", lambda e: e.activation(gtmp[0:4, 16:24], gtmp[0:4, 0:8], AF.Square))
    S.step("vector", lambda e: e.tensor_sub(gtmp[0:4, 24:32], gtmp[0:4, 8:16],
                                            gtmp[0:4, 16:24]))
    S.step("scalar", lambda e: e.activation(gtmp[0:4, 32:40], gtmp[0:4, 24:32],
                                            AF.Ln, bias=fc_sb[0:4, 1:2]))
    S.step("scalar", lambda e: e.activation(gtmp[0:4, 40:48], gtmp[0:4, 32:40],
                                            AF.Exp, scale=-0.5))
    S.step("tensor", lambda e: e.matmul(P0[:, 0:8], i32T[:], gtmp[0:4, 0:8],
                                        start=True, stop=True))
    S.step("tensor", lambda e: e.matmul(P0[:, 8:16], i32T[:], gtmp[0:4, 40:48],
                                        start=True, stop=True))
    S.step("scalar", lambda e: e.copy(s1r, P0[:, 0:8]))
    S.step("scalar", lambda e: e.copy(q1r, P0[:, 8:16]))
    t_ = tap("gnraw", [128, 32])
    if t_ is not None:
        S.dma(lambda e, t_=t_: e.dma_start(out=t_[:, 0:8], in_=s1r))
        S.dma(lambda e, t_=t_: e.dma_start(out=t_[:, 8:16], in_=q1r))
        S.dma(lambda e, t_=t_: e.dma_start(out=t_[:, 16:24], in_=gam_s[:]))
        S.dma(lambda e, t_=t_: e.dma_start(out=t_[:, 24:32], in_=bet_s[:]))
    t_ = tap("gtmp", [4, 64])
    if t_ is not None:
        S.dma(lambda e, t_=t_: e.dma_start(out=t_[:], in_=gtmp))
    S.step("vector", lambda e: e.tensor_mul(ga[:], q1r, gam_s[:]))
    S.step("vector", lambda e: e.tensor_mul(gb[:], s1r, ga[:]))
    S.step("vector", lambda e: e.tensor_sub(gb[:], bet_s[:], gb[:]))
    t_ = tap("xn1", [D, T], BF16)
    if t_ is not None:
        S.dma(lambda e, t_=t_: e.dma_start(
            out=t_.rearrange("(c p) t -> p c t", p=128), in_=xn1[:]))

    A.reset()
    nq = A.take([16, T], F32)
    nk = A.take([16, T], F32)
    m_end = A.off
    A.reset(m_end)
    hg = A.take([128, 8, TTS], BF16)
    sqs2 = A.take([128, 8, TTS], BF16)
    wpcq = A.take([128, 2, 8, 128], BF16)
    wpck = A.take([128, 2, 8, 128], BF16)
    wpv = A.take([128, 2, 8, 512], BF16)
    scr2.reset()
    mrow = scr2.take([16, T], F32)
    indHs = scr2.take([128, 8, 16], BF16)
    cval = scr2.take([16, 4], F32)
    cvalb = scr2.take([16, 1], BF16)
    crow = scr2.take([1, 16], F32)
    entry = S.all_points()
    started = set()

    def w(ch, pts):
        pts = [p for p in pts if p is not None]
        if ch not in started:
            started.add(ch)
            pts = pts + entry
        return pts

    entry = entry + [p_prep_q1]
    wre = wqkv1_bf.rearrange("(kc p) j -> p kc j", p=128)
    S.dma(lambda e: e.dma_start(out=indHs, in_=indH.rearrange("c p h -> p c h")),
          ch="wdma", waits=w("wdma", []))
    p_wv = None
    for fc in range(2):
        p_wv = S.dma(lambda e, fc=fc: e.dma_start(
            out=wpv[:, fc], in_=wre[:, :, 2048 + fc * 512:2048 + (fc + 1) * 512]),
            ch="wdma", waits=[])
    p_v_last = None
    p_vc = None
    p_vc2 = None
    p_qe = {}
    p_ke = {}
    p_q = {}
    p_k = {}
    p_nke = None
    p_nkm = None
    for tt in range(TTN):
        tsl = slice(tt * TTS, (tt + 1) * TTS)
        p_hg = S.step("vector", [
            (lambda e, c=c, tsl=tsl: e.tensor_scalar(
                hg[:, c, :], xn1[:, c, tsl], ga[:, c:c + 1], gb[:, c:c + 1],
                ALU.mult, ALU.add)) for c in range(8)],
            ch="dve", waits=w("dve", [p_v_last]))
        for oc in range(8):
            pb = oc % 2
            p_wq = S.dma(lambda e, oc=oc, pb=pb: e.dma_start(
                out=wpcq[:, pb], in_=wre[:, :, oc * 128:(oc + 1) * 128]),
                ch=f"fwq{pb}", waits=w(f"fwq{pb}", [p_q.get(oc - 2, p_q.get(oc + 6))]))
            p_wk = S.dma(lambda e, oc=oc, pb=pb: e.dma_start(
                out=wpck[:, pb], in_=wre[:, :, 1024 + oc * 128:1024 + (oc + 1) * 128]),
                ch=f"fwk{pb}", waits=w(f"fwk{pb}", [p_k.get(oc - 2, p_k.get(oc + 6))]))
            p_q[oc] = S.step("tensor", [
                (lambda e, kc=kc, pb=pb: e.matmul(
                    P0[:, 0:512], wpcq[:, pb, kc, :], hg[:, kc, :],
                    start=(kc == 0), stop=(kc == 7))) for kc in range(8)],
                ch="pe", waits=w("pe", [p_wq, p_hg, p_nke,
                                        p_qe.get(oc - 1, p_qe.get(oc + 7))]))
            p_qe[oc] = S.step("scalar", lambda e, oc=oc, tsl=tsl: e.activation(
                q1[:, oc, tsl], P0[:, 0:512], AF.Identity, scale=SCALE,
                bias=bqs[:, oc:oc + 1]),
                ch="act", waits=w("act", [p_q[oc]]))
            p_k[oc] = S.step("tensor", [
                (lambda e, kc=kc, pb=pb: e.matmul(
                    P0[:, 512:1024], wpck[:, pb, kc, :], hg[:, kc, :],
                    start=(kc == 0), stop=(kc == 7))) for kc in range(8)],
                ch="pe", waits=[p_wk, p_ke.get(oc - 1, p_ke.get(oc + 7))])
            p_ke[oc] = S.step("scalar", lambda e, oc=oc, tsl=tsl: e.activation(
                k1[:, oc, tsl], P0[:, 512:1024], AF.Identity,
                bias=bks[:, oc:oc + 1]),
                ch="act", waits=[p_k[oc]])
        for fc in range(2):
            for tb in range(4):
                tbg = tt * 4 + tb
                vb = (fc * 4 + tb) % 2
                mms = [
                    (lambda e, kc=kc, tb=tb, fc=fc, vb=vb: e.matmul(
                        P1[:, vb * 512:vb * 512 + 512],
                        hg[:, kc, tb * 128:(tb + 1) * 128], wpv[:, fc, kc, :],
                        start=(kc == 0), stop=False)) for kc in range(8)]
                mms.append(lambda e, fc=fc, vb=vb: e.matmul(
                    P1[:, vb * 512:vb * 512 + 512], ones_bf[0:1, 0:128],
                    bv_bf[0:1, fc * 512:(fc + 1) * 512],
                    start=False, stop=True))
                p_v = S.step("tensor", mms, ch="pe", waits=[p_vc2, p_nkm, p_wv])
                p_vc2 = p_vc
                p_vc = S.step("vector", lambda e, fc=fc, tbg=tbg, vb=vb: e.tensor_copy(
                    vaug1[:, tbg, fc * 520:fc * 520 + 520].rearrange(
                        "p (h v) -> p h v", v=65)[:, :, 0:64],
                    P1[:, vb * 512:vb * 512 + 512].rearrange("p (h v) -> p h v", v=64)),
                    ch="dve", waits=[p_v])
        p_v_last = p_v
        p_sqq = S.step("scalar", [
            (lambda e, c=c, tsl=tsl: e.activation(
                sqs2[:, c, :], q1[:, c, tsl], AF.Square)) for c in range(8)],
            ch="act", waits=[p_qe[7], p_nkm])
        p_nqm = S.step("tensor", [
            (lambda e, c=c: e.matmul(P0[0:16, 0:512], indHs[:, c, :], sqs2[:, c, :],
                                     start=(c == 0), stop=(c == 7))) for c in range(8)],
            ch="pe", waits=[p_sqq, p_qe[7], p_wv])
        p_nqe = S.step("scalar", lambda e, tsl=tsl: e.copy(nq[:, tsl], P0[0:16, 0:512]),
                       ch="act", waits=[p_nqm])
        p_sqk = S.step("scalar", [
            (lambda e, c=c, tsl=tsl: e.activation(
                sqs2[:, c, :], k1[:, c, tsl], AF.Square)) for c in range(8)],
            ch="act", waits=[p_nqm, p_ke[7]])
        p_nkm = S.step("tensor", [
            (lambda e, c=c: e.matmul(P0[0:16, 0:512], indHs[:, c, :], sqs2[:, c, :],
                                     start=(c == 0), stop=(c == 7))) for c in range(8)],
            ch="pe", waits=[p_sqk, p_nqe])
        p_nke = S.step("scalar", lambda e, tsl=tsl: e.copy(nk[:, tsl], P0[0:16, 0:512]),
                       ch="act", waits=[p_nkm])
    S.step("vector", lambda e: e.reduce_max(cval[:, 0:1], nk, axis=AX.X))
    S.step("scalar", lambda e: e.activation(cval[:, 1:2], cval[:, 0:1], AF.Ln))
    S.step("vector", lambda e: e.tensor_scalar_mul(cval[:, 2:3], cval[:, 1:2], 0.5))
    S.step("scalar", lambda e: e.activation(mrow, nq, AF.Ln))
    S.step("scalar", lambda e: e.activation(mrow, mrow, AF.Exp, scale=0.5,
                                            bias=cval[:, 2:3]))
    for qh in range(2):
        S.step("vector", lambda e, qh=qh: e.tensor_reduce(
            cval[:, 3:4], mrow[:, qh * 1024:(qh + 1) * 1024], AX.X, ALU.max,
            negate=True))
        S.step("vector", lambda e: e.tensor_copy(cvalb, cval[:, 3:4]))
        S.step("tensor", lambda e: e.transpose(
            P1[:].bitcast(BF16)[0:1, 0:16], cvalb, idb[0:16, 0:16]))
        S.step("scalar", lambda e: e.copy(crow, P1[:].bitcast(BF16)[0:1, 0:16]))
        S.step("tensor", lambda e: e.matmul(P0[:, 0:16], ones_f[0:1, 0:128], crow,
                                            start=True, stop=True))
        S.step("scalar", lambda e, qh=qh: e.copy(cneg[:, qh * 16:qh * 16 + 16], P0[:, 0:16]))

    t_ = tap("q1", [D, T], BF16)
    if t_ is not None:
        S.dma(lambda e, t_=t_: e.dma_start(
            out=t_.rearrange("(c p) t -> p c t", p=128), in_=q1[:]))
    t_ = tap("hgw", [D, HID], BF16)
    if t_ is not None:
        S.dma(lambda e, t_=t_: e.dma_start(out=t_[:], in_=wqkv1_bf[:]))
    t_ = tap("nq", [16, T])
    if t_ is not None:
        S.dma(lambda e, t_=t_: e.dma_start(out=t_[:], in_=nq))
    t_ = tap("mrow", [16, T])
    if t_ is not None:
        S.dma(lambda e, t_=t_: e.dma_start(out=t_[:], in_=mrow))
    t_ = tap("cneg", [128, 32])
    if t_ is not None:
        S.dma(lambda e, t_=t_: e.dma_start(out=t_[:], in_=cneg[:]))

    A.reset()
    pbf = A.take([128, 2, 1024], BF16)
    srow = A.take([128, 16], F32)
    rrow = A.take([128, 16], F32)
    abuf = A.take([128, 16, 128], BF16)
    qz = A.take([128, 2, 2048], BF16)
    entry = S.all_points()
    started = set()

    def w(ch, pts):
        pts = [p for p in pts if p is not None]
        if ch not in started:
            started.add(ch)
            pts = pts + entry
        return pts

    p_z = S.step("vector", [
        lambda e: e.memset(qz[64:128, 0, :], 0.0),
        lambda e: e.memset(qz[0:64, 1, :], 0.0)], ch="dve", waits=w("dve", []))

    def qk_mm(h, kc, qh):
        hp = h // 2
        p_s = P0 if qh == 0 else P1
        return [(lambda e, j=j, kc=kc, qh=qh, hp=hp, h=h, p_s=p_s: e.matmul(
            p_s[:, j * 512:(j + 1) * 512],
            k1[:, hp, kc * 128:(kc + 1) * 128],
            qz[:, h % 2, qh * 1024 + j * 512:qh * 1024 + (j + 1) * 512],
            start=True, stop=True)) for j in range(2)]

    def av_mm(h, kc, qh):
        return [(lambda e, qc=qc, kc=kc, qh=qh, h=h: e.matmul(
            avslot(qh * 8 + qc), pbf[:, qh, qc * 128:(qc + 1) * 128],
            vaug1[:, kc, 65 * h:65 * h + 65],
            start=(kc == 0 and (qh * 8 + qc) in (0, 7, 14)),
            stop=(kc == 15))) for qc in range(8)]

    def fill_mm(n):
        return [(lambda e: e.matmul(
            P0[:, 0:512], ones_bf[:, 0:128], pbf[:, 0, 0:512],
            start=True, stop=True)) for _ in range(n)]

    s_war = {0: None, 1: None}
    p_div = None
    p_tdma = None
    p_qz = {}
    p_qk_last = {}
    for hp in range(8):
        for h2 in range(2):
            h = 2 * hp + h2
            hb = 64 * h2
            p_qz[h2] = S.step("vector", lambda e, hb=hb, hp=hp, h2=h2:
                              e.tensor_copy(qz[hb:hb + 64, h2, :],
                                            q1[hb:hb + 64, hp, :]),
                              ch="dve", waits=[p_qk_last.get(h2)])
            for kc in range(16):
                for qh in range(2):
                    ops = []
                    ops += (av_mm(h, kc - 1, qh) if kc > 0 else []) + qk_mm(h, kc, qh)
                    pw = [s_war[qh]]
                    if kc == 0:
                        pw.append(p_qz[h2])
                    if kc == 1:
                        pw.append(p_div)
                    p_pe = S.step("tensor", ops, ch="pe", waits=w("pe", pw))
                    s_war[qh] = S.step("scalar", lambda e, qh=qh, h=h: e.activation(
                        pbf[:, qh, :], (P0 if qh == 0 else P1)[:], AF.Exp,
                        bias=cneg[:, qh * 16 + h:qh * 16 + h + 1]),
                        ch="act", waits=w("act", [p_pe]))
            p_qk_last[h2] = p_pe
            p_pe = S.step("tensor", av_mm(h, 15, 0) + av_mm(h, 15, 1),
                          ch="pe", waits=[s_war[0], s_war[1]])
            p_sr = S.step("vector", [
                lambda e: e.tensor_copy(
                    srow[:, 0:7],
                    P2[:, 0:455].rearrange("p (a v) -> p a v", v=65)[:, :, 64:65].squeeze(2)),
                lambda e: e.tensor_copy(
                    srow[:, 7:14],
                    P2[:, 512:967].rearrange("p (a v) -> p a v", v=65)[:, :, 64:65].squeeze(2)),
                lambda e: e.tensor_copy(
                    srow[:, 14:16],
                    P3[:, 0:130].rearrange("p (a v) -> p a v", v=65)[:, :, 64:65].squeeze(2)),
            ], ch="dve", waits=w("dve", [p_pe]))
            p_a = S.step("scalar", lambda e: e.activation(srow[:], srow[:], AF.Ln),
                         ch="act", waits=[p_sr])
            p_a = S.step("scalar", lambda e: e.activation(rrow[:], srow[:], AF.Exp,
                                                          scale=-1.0),
                         ch="act", waits=[p_a])
            hb = 64 * h2
            p_div = S.step("vector", [
                (lambda e, qt=qt, hb=hb: e.tensor_scalar_mul(
                    abuf[:, qt, hb:hb + 64], avslot(qt)[:, 0:64],
                    rrow[:, qt:qt + 1])) for qt in range(16)],
                ch="dve", waits=[p_a] + ([p_tdma] if h2 == 0 else []))
        for qt in range(16):
            p_tdma = S.dma(lambda e, qt=qt, hp=hp: e.dma_start_transpose(
                k1[:, hp, qt * 128:(qt + 1) * 128], abuf[:, qt, :]),
                ch="tdma", waits=(w("tdma", [p_div]) if qt == 0 else []))
    t_ = tap("atm", [D, T], BF16)
    if t_ is not None:
        S.dma(lambda e, t_=t_: e.dma_start(
            out=t_.rearrange("(c p) t -> p c t", p=128), in_=k1[:]))

    A.reset()
    w1s = A.take([128, 8, D], BF16)
    wo1s = A.take([128, 8, D], BF16)
    cts = A.take([32, D], BF16)
    vrs = A.take([128, 256], BF16)
    t1b = A.take([32, T], BF16)
    ts1 = A.take([128, TTS], F32)
    ts2 = A.take([128, TTS], F32)
    ts3 = A.take([128, TTS], F32)
    wst2 = A.take([128, 2, 1024], F32)
    entry = S.all_points()
    started = set()

    def w(ch, pts):
        pts = [p for p in pts if p is not None]
        if ch not in started:
            started.add(ch)
            pts = pts + entry
        return pts

    p_cp = {}
    loads = [(w1T, w1s, c) for c in range(8)] + [(wo1, wo1s, c) for c in range(8)]
    for i, (srcb, dstb, c) in enumerate(loads):
        pb = i % 2
        p_l = S.dma(lambda e, srcb=srcb, c=c, pb=pb: e.dma_start(
            out=wst2[:, pb, :], in_=srcb[c * 128:(c + 1) * 128, :]),
            ch=f"hwl{pb}", waits=w(f"hwl{pb}", [p_cp.get(i - 2)]))
        p_cp[i] = S.step("vector", lambda e, dstb=dstb, c=c, pb=pb: e.tensor_copy(
            dstb[:, c, :], wst2[:, pb, :]), ch="dve", waits=w("dve", [p_l]))
    p_l = S.dma(lambda e: e.dma_start(out=wst2[0:32, 0, :], in_=cT[:]),
                ch="wdma", waits=[p_cp[14]])
    p_ct = S.step("vector", lambda e: e.tensor_copy(cts[:], wst2[0:32, 0, :]),
                  ch="dve", waits=[p_l])
    p_l = S.dma(lambda e: e.dma_start(
        out=wst2[:, 1, 0:256].rearrange("p (c r) -> p c r", r=32),
        in_=vr.rearrange("(c p) r -> p c r", p=128)),
        ch="wdma", waits=[p_cp[15]])
    p_vr = S.step("vector", lambda e: e.tensor_copy(vrs[:], wst2[:, 1, 0:256]),
                  ch="dve", waits=[p_l])
    p_t1e = None
    for tt in range(TTN):
        tsl = slice(tt * TTS, (tt + 1) * TTS)
        tb1 = (tt % 2) * 512
        p_t1 = S.step("tensor", [
            (lambda e, kc=kc, tsl=tsl, tb1=tb1: e.matmul(
                P1[0:32, tb1:tb1 + 512], vrs[:, kc * 32:(kc + 1) * 32],
                xn1[:, kc, tsl],
                start=(kc == 0), stop=(kc == 7))) for kc in range(8)],
            ch="pe", waits=w("pe", [p_vr, p_t1e]))
        p_t1e = S.step("scalar", lambda e, tsl=tsl, tb1=tb1: e.copy(
            t1b[:, tsl], P1[0:32, tb1:tb1 + 512]),
            ch="act", waits=w("act", [p_t1]))
    p_evac = {}
    p_y = None
    p_add1 = None
    i = 0
    for tt in range(TTN):
        tsl = slice(tt * TTS, (tt + 1) * TTS)
        for oc in range(8):
            eb = (i % 2) * 512
            mms = [
                (lambda e, kc=kc, oc=oc, tsl=tsl, eb=eb: e.matmul(
                    P2[:, eb:eb + 512], w1s[:, kc, oc * 128:(oc + 1) * 128],
                    xn1[:, kc, tsl],
                    start=(kc == 0), stop=False)) for kc in range(8)]
            mms.append(lambda e, oc=oc, tsl=tsl, eb=eb: e.matmul(
                P2[:, eb:eb + 512], cts[0:32, oc * 128:(oc + 1) * 128],
                t1b[0:32, tsl],
                start=False, stop=False))
            mms += [
                (lambda e, vc=vc, oc=oc, tsl=tsl, eb=eb: e.matmul(
                    P2[:, eb:eb + 512], wo1s[:, vc, oc * 128:(oc + 1) * 128],
                    k1[:, vc, tsl],
                    start=False, stop=(vc == 7))) for vc in range(8)]
            p_mm = S.step("tensor", mms, ch="pe",
                          waits=[p_evac.get(i - 2), p_t1e, p_ct])
            p_x = S.dma(lambda e, oc=oc, tsl=tsl: e.dma_start(
                out=ts2, in_=x.rearrange("(c p) t -> p c t", p=128)[:, oc, tsl]),
                ch="xdma", waits=w("xdma", [p_add1]))
            p_evac[i] = S.step("scalar", lambda e, oc=oc, eb=eb: e.activation(
                ts1, P2[:, eb:eb + 512], AF.Identity, bias=bsum[:, oc:oc + 1]),
                ch="act", waits=[p_mm, p_y])
            p_add1 = S.step("vector", lambda e: e.tensor_add(ts3, ts1, ts2),
                            ch="dve", waits=[p_evac[i], p_x])
            p_add2 = S.step("vector", lambda e, oc=oc, tsl=tsl: e.tensor_add(
                ts1, ts3, xn1[:, oc, tsl]), ch="dve", waits=[])
            p_y = S.dma(lambda e, oc=oc, tsl=tsl: e.dma_start(
                out=y.rearrange("(c p) t -> p c t", p=128)[:, oc, tsl], in_=ts1),
                ch="st", waits=w("st", [p_add2]))
            i += 1


def build_l1(taps=()):
    nc = bass.Bass("TRN2", target_bir_lowering=False)
    es = ExitStack()
    S = Sched(nc, es)

    x = nc.dram_tensor("x", [D, T], F32, kind="ExternalInput")
    wqkv1 = nc.dram_tensor("wqkv1", [D, HID], F32, kind="ExternalInput")
    bqkv1 = nc.dram_tensor("bqkv1", [HID], F32, kind="ExternalInput")
    wo1 = nc.dram_tensor("wo1", [D, D], F32, kind="ExternalInput")
    bo1 = nc.dram_tensor("bo1", [D], F32, kind="ExternalInput")
    gam = nc.dram_tensor("gam", [D], F32, kind="ExternalInput")
    bet = nc.dram_tensor("bet", [D], F32, kind="ExternalInput")
    w1T = nc.dram_tensor("w1T", [D, D], F32, kind="ExternalInput")
    b1 = nc.dram_tensor("b1", [D], F32, kind="ExternalInput")
    vr = nc.dram_tensor("vr", [D, 32], F32, kind="ExternalInput")
    cT = nc.dram_tensor("cT", [32, D], F32, kind="ExternalInput")
    indH = nc.dram_tensor("indH", [8, 128, 16], BF16, kind="ExternalInput")
    ind32 = nc.dram_tensor("ind32", [128, 4], F32, kind="ExternalInput")
    ind32T = nc.dram_tensor("ind32T", [4, 128], F32, kind="ExternalInput")
    ident = nc.dram_tensor("ident", [128, 128], BF16, kind="ExternalInput")
    fconst = nc.dram_tensor("fconst", [128, 8], F32, kind="ExternalInput")
    y = nc.dram_tensor("y", [D, T], F32, kind="ExternalOutput")
    wqkv1_bf = nc.dram_tensor("wqkv1_bf", [D, HID], BF16)

    dbg = {}

    def tap(name, shape, dt=F32):
        if name in taps:
            dbg[name] = nc.dram_tensor("dbg_" + name, shape, dt, kind="ExternalOutput")
        return dbg.get(name)

    def sbt(name, shape, dt):
        return es.enter_context(nc.sbuf_tensor(name, shape, dt))

    xn1 = sbt("xn1", [128, 8, T], BF16)
    q1 = sbt("q1", [128, 8, T], BF16)
    k1 = sbt("k1", [128, 8, T], BF16)
    vaug1 = sbt("vaug1", [128, 16, 1040], BF16)
    idb = sbt("idb", [128, 128], BF16)
    ones_bf = sbt("ones_bf", [128, 128], BF16)
    ones_f = sbt("ones_f", [1, 128], F32)
    i32 = sbt("i32", [128, 4], F32)
    i32T = sbt("i32T", [4, 128], F32)
    ga = sbt("ga", [128, 8], F32)
    gb = sbt("gb", [128, 8], F32)
    bqs = sbt("bqs", [128, 8], F32)
    bks = sbt("bks", [128, 8], F32)
    bv_bf = sbt("bv_bf", [1, 1024], BF16)
    bsum = sbt("bsum", [128, 8], F32)
    cneg = sbt("cneg", [128, 32], F32)
    fc_sb = sbt("fc_sb", [128, 8], F32)
    gam_s = sbt("gam_s", [128, 8], F32)
    bet_s = sbt("bet_s", [128, 8], F32)
    ar_t = sbt("arena", [128, 14336], F32)
    A = Arena(ar_t, 56)
    spare = sbt("spare", [128, 7936], BF16)
    scr2 = Arena(spare, 15)

    P0 = es.enter_context(nc.psum_tensor("P0", [128, 1024], F32))
    P1 = es.enter_context(nc.psum_tensor("P1", [128, 1024], F32))
    P2 = es.enter_context(nc.psum_tensor("P2", [128, 1024], F32))
    P3 = es.enter_context(nc.psum_tensor("P3", [128, 1024], F32))

    emit_l1(nc, es, S, tap, A, P0, P1, P2, P3, xn1, q1, k1, vaug1, idb, ones_bf, ones_f, i32, i32T, ga, gb, bqs, bks, bv_bf, bsum, cneg, fc_sb, gam_s, bet_s, x, wqkv1, bqkv1, wo1, bo1, gam, bet, w1T, b1, vr, cT, indH, ind32, ind32T, ident, fconst, y, wqkv1_bf, scr2)
    S.emit()
    return nc, es




def build_full(taps=()):
    nc = bass.Bass("TRN2", target_bir_lowering=False)
    es = ExitStack()
    S = Sched(nc, es)

    x = nc.dram_tensor("x", [D, T], F32, kind="ExternalInput")
    wqkvT = nc.dram_tensor("wqkvT", [D, HID], F32, kind="ExternalInput")
    wguT = nc.dram_tensor("wguT", [24, 128, 2048], I8, kind="ExternalInput")
    wdT = nc.dram_tensor("wdT", [HID, D], I8, kind="ExternalInput")
    woT = nc.dram_tensor("woT", [D, D], F32, kind="ExternalInput")
    cosq = nc.dram_tensor("cosq", [128, T], BF16, kind="ExternalInput")
    sinq = nc.dram_tensor("sinq", [128, T], BF16, kind="ExternalInput")
    cosk = nc.dram_tensor("cosk", [T, 512], BF16, kind="ExternalInput")
    sink = nc.dram_tensor("sink", [T, 512], BF16, kind="ExternalInput")
    ind_den = nc.dram_tensor("ind_den", [8, 128, 16], BF16, kind="ExternalInput")
    ind3 = nc.dram_tensor("ind3", [8, 16, 128], BF16, kind="ExternalInput")
    fconst = nc.dram_tensor("fconst", [128, 8], F32, kind="ExternalInput")
    wqkv1 = nc.dram_tensor("wqkv1", [D, HID], F32, kind="ExternalInput")
    bqkv1 = nc.dram_tensor("bqkv1", [HID], F32, kind="ExternalInput")
    wo1 = nc.dram_tensor("wo1", [D, D], F32, kind="ExternalInput")
    bo1 = nc.dram_tensor("bo1", [D], F32, kind="ExternalInput")
    gam = nc.dram_tensor("gam", [D], F32, kind="ExternalInput")
    bet = nc.dram_tensor("bet", [D], F32, kind="ExternalInput")
    w1T = nc.dram_tensor("w1T", [D, D], F32, kind="ExternalInput")
    b1 = nc.dram_tensor("b1", [D], F32, kind="ExternalInput")
    vr = nc.dram_tensor("vr", [D, 32], F32, kind="ExternalInput")
    cT = nc.dram_tensor("cT", [32, D], F32, kind="ExternalInput")
    indH = nc.dram_tensor("indH", [8, 128, 16], BF16, kind="ExternalInput")
    ind32 = nc.dram_tensor("ind32", [128, 4], F32, kind="ExternalInput")
    ind32T = nc.dram_tensor("ind32T", [4, 128], F32, kind="ExternalInput")
    ident = nc.dram_tensor("ident", [128, 128], BF16, kind="ExternalInput")
    y = nc.dram_tensor("y", [D, T], F32, kind="ExternalOutput")
    xmid = nc.dram_tensor("xmid", [D, T], F32)
    wqkv1_bf = nc.dram_tensor("wqkv1_bf", [D, HID], BF16)

    dbg = {}

    def tap(name, shape, dt=F32):
        if name in taps:
            dbg[name] = nc.dram_tensor("dbg_" + name, shape, dt, kind="ExternalOutput")
        return dbg.get(name)

    def sbt(name, shape, dt):
        return es.enter_context(nc.sbuf_tensor(name, shape, dt))

    big0 = sbt("big0", [128, 8, T], BF16)
    big1 = sbt("big1", [128, 8, T], BF16)
    big2 = sbt("big2", [128, 8, T], BF16)
    wbig = sbt("wbig", [128, 24576], BF16)
    ar_t = sbt("arena", [128, 14336], F32)
    ones_bf = sbt("ones_bf", [128, 128], BF16)
    ones_f = sbt("ones_f", [1, 128], F32)
    fc_sb = sbt("fc_sb", [128, 8], F32)
    kv_sb = sbt("kv_sb", [128, 1024], BF16)
    u_sb = sbt("u_sb", [128, 8], F32)
    idb = sbt("idb", [128, 128], BF16)
    i32 = sbt("i32", [128, 4], F32)
    i32T = sbt("i32T", [4, 128], F32)
    ga = sbt("ga", [128, 8], F32)
    gb = sbt("gb", [128, 8], F32)
    bqs = sbt("bqs", [128, 8], F32)
    bks = sbt("bks", [128, 8], F32)
    bv_bf = sbt("bv_bf", [1, 1024], BF16)
    bsum = sbt("bsum", [128, 8], F32)
    cneg = sbt("cneg", [128, 32], F32)
    gam_s = sbt("gam_s", [128, 8], F32)
    bet_s = sbt("bet_s", [128, 8], F32)

    P0 = es.enter_context(nc.psum_tensor("P0", [128, 1024], F32))
    P1 = es.enter_context(nc.psum_tensor("P1", [128, 1024], F32))
    P2 = es.enter_context(nc.psum_tensor("P2", [128, 1024], F32))
    P3 = es.enter_context(nc.psum_tensor("P3", [128, 1024], F32))

    A0 = Arena(ar_t, 40)
    woA = Arena(ar_t, 56)
    woA.reset(40 * 1024)
    wo_sb = woA.take([128, 8, 1024], BF16)
    emit_l0(nc, es, S, tap, A0, P0, P1, P2, P3, big0, big1, big2, wbig, wo_sb,
            kv_sb, u_sb, ones_bf, ones_f, fc_sb, x, wqkvT, wguT, wdT, woT,
            cosq, sinq, cosk, sink, ind_den, ind3, fconst, xmid)

    A1 = Arena(ar_t, 56)
    vaug1 = wbig[:, 0:16640].rearrange("p (a b) -> p a b", b=1040)
    scr2 = Arena(wbig[:, 16640:24576], 15)
    emit_l1(nc, es, S, tap, A1, P0, P1, P2, P3, big0, big1, big2, vaug1, idb,
            ones_bf, ones_f, i32, i32T, ga, gb, bqs, bks, bv_bf, bsum, cneg,
            fc_sb, gam_s, bet_s, xmid, wqkv1, bqkv1, wo1, bo1, gam, bet, w1T,
            b1, vr, cT, indH, ind32, ind32T, ident, fconst, y, wqkv1_bf, scr2)

    S.emit()
    return nc, es



def _host_consts():
    c = {}
    inv = 10000.0 ** (-np.arange(32, dtype=np.float64) / 32.0)
    ang = np.outer(np.arange(T, dtype=np.float64), inv)
    cos, sin = np.cos(ang), np.sin(ang)
    pp = np.arange(128) % 32
    c["cosq"] = np.ascontiguousarray((cos[:, pp].T * SCALE)).astype(BF)
    c["sinq"] = np.ascontiguousarray((sin[:, pp].T * SCALE)).astype(BF)
    jj = np.arange(512) % 32
    c["cosk"] = np.ascontiguousarray(cos[:, jj]).astype(BF)
    c["sink"] = np.ascontiguousarray(sin[:, jj]).astype(BF)
    ind_den = np.zeros((8, 128, 16), np.float32)
    for ch in range(8):
        for p in range(128):
            r = ch * 128 + p
            hh = r // 32 if r < 512 else (r - 512) // 32
            ind_den[ch, p, hh] = 1.0
    c["ind_den"] = ind_den.astype(BF)
    ind3 = np.zeros((8, 16, 128), np.float32)
    for ch in range(8):
        ind3[ch, 2 * ch, 0:64] = 1.0
        ind3[ch, 2 * ch + 1, 64:128] = 1.0
    c["ind3"] = ind3.astype(BF)
    c["indH"] = np.ascontiguousarray(ind3.transpose(0, 2, 1)).astype(BF)
    g32 = np.zeros((128, 4), np.float32)
    g32[np.arange(128), np.arange(128) // 32] = 1.0
    c["ind32"] = g32
    c["ind32T"] = np.ascontiguousarray(g32.T)
    c["ident"] = np.eye(128, dtype=np.float32).astype(BF)
    fc = np.zeros((128, 8), np.float32)
    fc[:, 0] = 1e-5
    fc[:, 1] = 1e-6
    fc[:, 2] = LN127SQ
    c["fconst"] = fc
    return c


def _prep_weights(inp):
    w = {}
    perm = np.concatenate([
        (np.arange(16)[:, None] * 64 + np.arange(32)[None, :]).reshape(-1),
        (np.arange(16)[:, None] * 64 + 32 + np.arange(32)[None, :]).reshape(-1)])
    wq = np.asarray(inp["L0_attn_w_qkv"], dtype=np.float32)
    wqkv = np.concatenate([wq[0:1024][perm], wq[1024:2048][perm], wq[2048:3072]], 0)
    w["wqkvT"] = np.ascontiguousarray(wqkv.T)
    def tile_ju(m):
        return np.ascontiguousarray(
            m.T.reshape(8, 128, 24, 128).transpose(2, 1, 0, 3))
    g8 = tile_ju(np.asarray(inp["L0_ffn_w_gate"]).astype(np.int8)).reshape(24, 128, 1024)
    u8 = tile_ju(np.asarray(inp["L0_ffn_w_up"]).astype(np.int8)).reshape(24, 128, 1024)
    w["wguT"] = np.ascontiguousarray(np.concatenate([g8, u8], axis=2))
    w["wdT"] = np.ascontiguousarray(np.asarray(inp["L0_ffn_w_down"]).astype(np.int8).T)
    w["woT"] = np.ascontiguousarray(np.asarray(inp["L0_attn_w_out"], dtype=np.float32).T)
    w["wqkv1"] = np.ascontiguousarray(np.asarray(inp["L1_attn_W_qkv"], dtype=np.float32))
    w["bqkv1"] = np.ascontiguousarray(np.asarray(inp["L1_attn_b_qkv"], dtype=np.float32))
    w["wo1"] = np.ascontiguousarray(np.asarray(inp["L1_attn_W_o"], dtype=np.float32))
    w["bo1"] = np.ascontiguousarray(np.asarray(inp["L1_attn_b_o"], dtype=np.float32))
    w["gam"] = np.ascontiguousarray(np.asarray(inp["L1_attn_gn_gamma"], dtype=np.float32))
    w["bet"] = np.ascontiguousarray(np.asarray(inp["L1_attn_gn_beta"], dtype=np.float32))
    w["w1T"] = np.ascontiguousarray(np.asarray(inp["L1_ffn_W"], dtype=np.float32).T)
    w["b1"] = np.ascontiguousarray(np.asarray(inp["L1_ffn_b"], dtype=np.float32))
    w["vr"] = np.ascontiguousarray(np.asarray(inp["L1_ffn_V_r"], dtype=np.float32))
    w["cT"] = np.ascontiguousarray(np.asarray(inp["L1_ffn_C"], dtype=np.float32).T)
    return w


def make_maps(inputs, consts=None, w=None):
    consts = consts or _host_consts()
    w = w or _prep_weights(inputs)
    x = np.asarray(inputs["L0_attn_x"], dtype=np.float32)
    shared0 = dict(wqkvT=w["wqkvT"], wguT=w["wguT"], wdT=w["wdT"],
                   woT=w["woT"], cosq=consts["cosq"], sinq=consts["sinq"],
                   cosk=consts["cosk"], sink=consts["sink"],
                   ind_den=consts["ind_den"], ind3=consts["ind3"],
                   fconst=consts["fconst"])
    shared1 = dict(wqkv1=w["wqkv1"], bqkv1=w["bqkv1"], wo1=w["wo1"], bo1=w["bo1"],
                   gam=w["gam"], bet=w["bet"], w1T=w["w1T"], b1=w["b1"],
                   vr=w["vr"], cT=w["cT"], indH=consts["indH"],
                   ind32=consts["ind32"], ind32T=consts["ind32T"],
                   ident=consts["ident"], fconst=consts["fconst"])
    maps0 = [dict(shared0, x=np.ascontiguousarray(x[b].T)) for b in range(8)]
    maps1 = [dict(shared1) for b in range(8)]
    return maps0, maps1


def kernel(**inputs):
    maps0, maps1 = make_maps(inputs)
    maps = [dict(maps0[b], **maps1[b]) for b in range(8)]
    nc, es = build_full()
    r = run_bass_kernel_spmd(nc, maps, core_ids=list(range(8)))
    out = np.stack([np.asarray(r.results[b]["y"]).T for b in range(8)], 0)
    return np.ascontiguousarray(out).astype(np.float32)


# version 46
# speedup vs baseline: 1.0094x; 1.0094x over previous
import math
import numpy as np
import ml_dtypes
import concourse.bass as bass
import concourse.mybir as mybir
from concourse.bass_utils import run_bass_kernel_spmd
from contextlib import ExitStack

F32 = mybir.dt.float32
BF16 = mybir.dt.bfloat16
I32 = mybir.dt.int32
I8 = mybir.dt.int8
AF = mybir.ActivationFunctionType
ALU = mybir.AluOpType
AX = mybir.AxisListType
BF = ml_dtypes.bfloat16

T = 2048
D = 1024
HID = 3072
NH = 16
TTN = 4
TTS = 512
SCALE = 0.125
DQ = 1.0 / 127.0
LN127SQ = 2.0 * math.log(127.0)
SEM_ROT = 20000
G_SERIAL = True
D_SERIAL = False


class Sched:

    def __init__(self, nc, es):
        self.nc, self.es = nc, es
        self.sems = []
        self.steps = []
        self.ch = {}
        self.bgset = set()
        self.last_eng = {}
        self._chan("main")

    def _new_sem(self):
        s = self.es.enter_context(self.nc.semaphore(f"g{len(self.sems)}"))
        self.sems.append(s)
        return len(self.sems) - 1

    def _chan(self, name):
        if name not in self.ch:
            self.ch[name] = [self._new_sem(), 0]
        return self.ch[name]

    def bg(self, name):
        self._chan(name)
        self.bgset.add(name)

    def point(self, ch="main"):
        si, cnt = self._chan(ch)
        return (si, cnt)

    def all_points(self):
        return [(si, cnt) for nm, (si, cnt) in self.ch.items()
                if nm not in self.bgset]

    def final_points(self):
        return [(si, cnt) for si, cnt in self.ch.values()]

    def step(self, eng, fns, dma=False, ch="main", waits="all"):
        if not isinstance(fns, (list, tuple)):
            fns = [fns]
        c = self._chan(ch)
        if c[1] >= SEM_ROT:
            old_si, old_cnt = c
            c[0] = self._new_sem()
            c[1] = 0
            self.steps.append(("BARRIER", old_si, old_cnt, None, None))
        if waits == "all":
            w = self.all_points()
        else:
            w = [p for p in waits if p is not None]
        if eng in ("vector", "scalar", "gpsimd") and (eng, ch) in self.last_eng:
            w = w + [self.last_eng[(eng, ch)]]
        q = 16 if dma else 1
        self.steps.append((eng, c[0], list(w), list(fns), q))
        c[1] += q * len(fns) if dma else 1
        self.last_eng[(eng, ch)] = (c[0], c[1])
        return (c[0], c[1])

    def dma(self, fn, ch="main", waits="all", eng="sync"):
        return self.step(eng, [fn], dma=True, ch=ch, waits=waits)

    def check(self):
        engines = {}
        for i, st in enumerate(self.steps):
            if st[0] == "BARRIER":
                for e in ("sync", "vector", "scalar", "tensor", "gpsimd"):
                    engines.setdefault(e, []).append(("B", st[1], st[2], i))
            else:
                engines.setdefault(st[0], []).append(("S", st, i))
        counts = {}
        heads = {e: 0 for e in engines}
        progress = True
        while progress:
            progress = False
            for e, q in engines.items():
                while heads[e] < len(q):
                    item = q[heads[e]]
                    if item[0] == "B":
                        if counts.get(item[1], 0) < item[2]:
                            break
                    else:
                        _, si, w, fns, qq = item[1]
                        if any(counts.get(ws, 0) < wv for ws, wv in w):
                            break
                        counts[si] = counts.get(si, 0) + (qq * len(fns) if qq == 16 else 1)
                    heads[e] += 1
                    progress = True
        stuck = {e: (len(q) - heads[e]) for e, q in engines.items() if heads[e] < len(q)}
        if stuck:
            detail = []
            for e, q in engines.items():
                if heads[e] < len(q):
                    item = q[heads[e]]
                    if item[0] == "B":
                        detail.append(f"{e}@step{item[3]}: BARRIER sem{item[1]} needs {item[2]} has {counts.get(item[1], 0)}")
                    else:
                        _, si, w, fns, qq = item[1]
                        missing = [(ws, wv, counts.get(ws, 0)) for ws, wv in w if counts.get(ws, 0) < wv]
                        detail.append(f"{e}@step{item[3]}: needs {missing}")
            raise AssertionError("DEADLOCK: " + "; ".join(detail[:5]) + f" stuck={stuck}")

    def emit(self):
        self.check()
        used = set(st[0] for st in self.steps if st[0] != "BARRIER")
        used.add("sync")
        nc, sems, steps = self.nc, self.sems, self.steps
        finals = self.final_points()

        def emit_for(e, name):
            seen = {}
            for st in steps:
                if st[0] == "BARRIER":
                    if seen.get(st[1], -1) < st[2]:
                        e.wait_ge(sems[st[1]], st[2])
                        seen[st[1]] = st[2]
                    continue
                if st[0] != name:
                    continue
                _, si, w, fns, q = st
                for wsi, wval in w:
                    if wval > 0 and seen.get(wsi, -1) < wval:
                        e.wait_ge(sems[wsi], wval)
                        seen[wsi] = wval
                for i, fn in enumerate(fns):
                    inst = fn(e)
                    if q == 16 or i == len(fns) - 1:
                        inst.then_inc(sems[si], q)
            if name == "sync":
                for wsi, wval in finals:
                    if wval > 0:
                        e.wait_ge(sems[wsi], wval)

        with nc.Block() as block:
            for eng_name in ("sync", "vector", "scalar", "tensor", "gpsimd"):
                if eng_name not in used:
                    continue
                def mk(nm):
                    def f(e):
                        emit_for(e, nm)
                    return f
                getattr(block, eng_name)(mk(eng_name))


class Arena:

    def __init__(self, t, kib):
        try:
            self.base = t[:]
        except TypeError:
            self.base = t
        if self.base.ndim == 3:
            self.base = self.base.rearrange("p a b -> p (a b)")
        elif self.base.ndim == 4:
            self.base = self.base.rearrange("p a b c -> p (a b c)")
        self.t = t
        self.cap = kib * 1024
        self.off = 0

    def reset(self, byte_off=0):
        self.off = byte_off

    def take(self, shape, dt):
        nel = 1
        for s in shape[1:]:
            nel *= s
        esz = {F32: 4, BF16: 2, I32: 4, I8: 1}[dt]
        start = (self.off + 31) & ~31
        self.off = start + nel * esz
        assert self.off <= self.cap, f"arena overflow {self.off} > {self.cap}"
        e0 = start // esz
        v = self.base.bitcast(dt)[0:shape[0], e0:e0 + nel]
        if len(shape) > 2:
            pat = "p (" + " ".join(f"d{i}" for i in range(1, len(shape))) + ") -> p " + \
                  " ".join(f"d{i}" for i in range(1, len(shape)))
            kw = {f"d{i}": shape[i] for i in range(2, len(shape))}
            v = v.rearrange(pat, **kw)
        return v


def _ln_rows(S, row, p_sum, p_sq, p_ba, p_bb, abc, bbc, ones_f, eps_ap):
    m, s2, t1, a1 = row[0:1, 0:512], row[0:1, 512:1024], row[0:1, 1024:1536], row[0:1, 1536:2048]
    S.step("scalar", lambda e: e.mul(m, p_sum, 1.0 / 1024.0))
    S.step("scalar", lambda e: e.mul(s2, p_sq, 1.0 / 1024.0))
    S.step("scalar", lambda e: e.activation(t1, m, AF.Square))
    S.step("vector", lambda e: e.tensor_sub(a1, s2, t1))
    S.step("scalar", lambda e: e.activation(t1, a1, AF.Ln, bias=eps_ap))
    S.step("scalar", lambda e: e.activation(s2, t1, AF.Exp, scale=-0.5))
    S.step("vector", lambda e: e.tensor_mul(t1, m, s2))
    S.step("vector", lambda e: e.tensor_scalar_mul(m, t1, -1.0))
    S.step("tensor", lambda e: e.matmul(p_ba, ones_f[0:1, 0:128], s2,
                                        start=True, stop=True))
    S.step("tensor", lambda e: e.matmul(p_bb, ones_f[0:1, 0:128], m,
                                        start=True, stop=True))
    S.step("scalar", lambda e: e.copy(abc[:], p_ba))
    S.step("scalar", lambda e: e.copy(bbc[:], p_bb))


def _ln_rows_p(S, w, row, p_sum, p_sq, p_ba, p_bb, abc, bbc, ones_f, eps_ap,
               first, prev_bc_readers):
    m, s2, t1, a1 = row[0:1, 0:512], row[0:1, 512:1024], row[0:1, 1024:1536], row[0:1, 1536:2048]
    pa1 = S.step("scalar", lambda e: e.mul(m, p_sum, 1.0 / 1024.0),
                 ch="act", waits=w("act", first))
    S.step("scalar", lambda e: e.mul(s2, p_sq, 1.0 / 1024.0), ch="act", waits=[])
    pa3 = S.step("scalar", lambda e: e.activation(t1, m, AF.Square), ch="act", waits=[])
    pd1 = S.step("vector", lambda e: e.tensor_sub(a1, s2, t1), ch="dve",
                 waits=w("dve", [pa3]))
    pa4 = S.step("scalar", lambda e: e.activation(t1, a1, AF.Ln, bias=eps_ap),
                 ch="act", waits=[pd1])
    pa5 = S.step("scalar", lambda e: e.activation(s2, t1, AF.Exp, scale=-0.5),
                 ch="act", waits=[])
    pd2 = S.step("vector", lambda e: e.tensor_mul(t1, m, s2), ch="dve", waits=[pa5])
    pd3 = S.step("vector", lambda e: e.tensor_scalar_mul(m, t1, -1.0), ch="dve", waits=[])
    pm1 = S.step("tensor", lambda e: e.matmul(p_ba, ones_f[0:1, 0:128], s2,
                                              start=True, stop=True),
                 ch="pe", waits=w("pe", [pa5] + prev_bc_readers))
    pm2 = S.step("tensor", lambda e: e.matmul(p_bb, ones_f[0:1, 0:128], m,
                                              start=True, stop=True),
                 ch="pe", waits=[pd3])
    pab = S.step("scalar", lambda e: e.copy(abc[:], p_ba), ch="act", waits=[pm1])
    pbb = S.step("scalar", lambda e: e.copy(bbc[:], p_bb), ch="act", waits=[pm2])
    return pab, pbb, pm2



def emit_l0(nc, es, S, tap, A, P0, P1, P2, P3, xn, qf, asc, wbig, wo_sb, kv_sb, u_sb, ones_bf, ones_f, fc_sb, x, wqkvT, wguT, wdT, woT, cosq, sinq, cosk, sink, ind_den, ind3, fconst, y):
    def wv(kc, lo, hi):
        return wbig[:, kc * 3072 + lo: kc * 3072 + hi]

    def wdv(jc, lo, hi):
        return wbig[:, jc * 1024 + lo: jc * 1024 + hi]

    def kvslot(h, eo):
        a = 32 * (h % 4)
        b = 2 * (h // 4) + eo
        col = 65 * b if b < 7 else 512
        return a, col

    S.step("vector", lambda e: e.memset(ones_bf[:], 1.0))
    S.step("vector", lambda e: e.memset(ones_f[:], 1.0))
    S.dma(lambda e: e.dma_start(out=fc_sb[:], in_=fconst[:]))

    S.bg("pdi0")
    S.bg("pdi1")
    S.bg("pcast")
    Aq = Arena(asc, 32)
    pst_f = [Aq.take([128, 1024], F32) for _ in range(2)]
    prep_state = {"i": 0, "pi": {}, "pc": {}, "done_qkv": None}
    all_jobs = [("q", i // 3, i % 3) for i in range(24)]

    def _prep_in(i):
        _, c, j = all_jobs[i]
        pb = i % 2
        st = prep_state
        st["pi"][i] = S.dma(lambda e, c=c, j=j, pb=pb: e.dma_start(
            out=pst_f[pb],
            in_=wqkvT[c * 128:(c + 1) * 128, j * 1024:(j + 1) * 1024]),
            ch=f"pdi{pb}", waits=[st["pc"].get(i - 2)])

    def _prep_fin(i):
        _, c, j = all_jobs[i]
        pb = i % 2
        st = prep_state
        st["pc"][i] = S.step("vector", lambda e, c=c, j=j, pb=pb: e.tensor_copy(
            wv(c, j * 1024, (j + 1) * 1024), pst_f[pb]),
            ch="pcast", waits=[st["pi"][i]])
        if i == len(all_jobs) - 1:
            st["done_qkv"] = st["pc"][i]

    def pump(n=1):
        st = prep_state
        for _ in range(n):
            site = st["i"]
            if site >= len(all_jobs) + 2:
                return
            if site - 2 >= 0:
                _prep_fin(site - 2)
            if site < len(all_jobs):
                _prep_in(site)
            st["i"] = site + 1

    A.reset()
    xst = A.take([128, 2, TTS], F32)
    xbf = A.take([128, 8, TTS], BF16)
    sqs = A.take([128, 2, TTS], BF16)
    row = A.take([1, 2048], F32)
    abc = A.take([128, TTS], F32)
    bbc = A.take([128, TTS], F32)
    tmul = A.take([128, 2, TTS], BF16)
    entry = S.all_points()
    started = set()

    def w(ch, pts):
        pts = [p for p in pts if p is not None]
        if ch not in started:
            started.add(ch)
            pts = pts + entry
        return pts

    xre = x.rearrange("(c p) t -> p c t", p=128)
    p_cp = {}
    p_sq = {}
    p_ms = p_mq = p_mu = None
    p_ap = {}
    pab = pbb = None
    for tt in range(TTN):
        tsl = slice(tt * TTS, (tt + 1) * TTS)
        for c in range(8):
            pb = c % 2
            p_x = S.dma(lambda e, c=c, tsl=tsl, pb=pb: e.dma_start(
                out=xst[:, pb, :], in_=xre[:, c, tsl]),
                ch=f"xdma{pb}", waits=w(f"xdma{pb}",
                                        [p_cp.get(c - 2, p_cp.get(c + 6)),
                                         p_sq.get(c - 2, p_sq.get(c + 6))]))
            p_cp[c] = S.step("vector", lambda e, c=c, pb=pb: e.tensor_copy(
                xbf[:, c, :], xst[:, pb, :]), ch="dve", waits=w("dve", [p_x]))
            p_sq[c] = S.step("scalar", lambda e, pb=pb: e.activation(
                sqs[:, pb, :], xst[:, pb, :], AF.Square),
                ch="act", waits=w("act", [p_x, p_mq]))
            p_ms = S.step("tensor", lambda e, c=c: e.matmul(
                P1[0:1, 0:512], ones_bf[:, 0:1], xbf[:, c, :],
                start=(c == 0), stop=(c == 7)),
                ch="pe", waits=w("pe", [p_cp[c]] + ([pab] if c == 0 else [])))
            p_mq = S.step("tensor", lambda e, c=c, pb=pb: e.matmul(
                P1[0:1, 512:1024], ones_bf[:, 0:1], sqs[:, pb, :],
                start=(c == 0), stop=(c == 7)),
                ch="pe", waits=[p_sq[c]])
            if c % 2:
                pump(1)
        pab, pbb, _ = _ln_rows_p(S, w, row, P1[0:1, 0:512], P1[0:1, 512:1024],
                                 P0[:, 0:512], P0[:, 512:1024], abc, bbc, ones_f,
                                 fc_sb[0:1, 0:1], [p_ms, p_mq],
                                 [pab, pbb])
        for c in range(8):
            pb = c % 2
            p_mu = S.step("vector", lambda e, c=c, pb=pb: e.tensor_mul(
                tmul[:, pb, :], xbf[:, c, :], abc[:]), ch="dve",
                waits=[pab])
            p_ap[c] = S.step("vector", lambda e, c=c, tsl=tsl, pb=pb: e.tensor_add(
                xn[:, c, tsl], tmul[:, pb, :], bbc[:]), ch="dve", waits=[pbb])
            if c % 2:
                pump(1)
    t_ = tap("xn", [D, T], BF16)
    if t_ is not None:
        S.dma(lambda e, t_=t_: e.dma_start(
            out=t_.rearrange("(c p) t -> p c t", p=128), in_=xn[:]))

    A.reset()
    cqs = A.take([128, T], BF16)
    sqt = A.take([128, T], BF16)
    ckb = A.take([128, 512], BF16)
    skb = A.take([128, 512], BF16)
    qstg = A.take([128, 8, TTS], BF16)
    qrp = A.take([128, 8, TTS], BF16)
    kst = A.take([128, 1024], BF16)
    krp = A.take([128, 1024], BF16)
    kf_tb = A.take([128, 1024], BF16)
    kesc1 = A.take([128, 1024], BF16)
    kesc2 = A.take([128, 1024], BF16)
    vaug = A.take([128, 1040], BF16)
    entry = S.all_points()
    started = set()

    def w(ch, pts):
        pts = [p for p in pts if p is not None]
        if ch not in started:
            started.add(ch)
            pts = pts + entry
        return pts

    while prep_state["done_qkv"] is None:
        pump(1)
    entry = entry + [prep_state["done_qkv"]]
    S.dma(lambda e: e.dma_start(out=cqs, in_=cosq[:]), ch="wdma",
          waits=w("wdma", []))
    p_tab = S.dma(lambda e: e.dma_start(out=sqt, in_=sinq[:]), ch="wdma", waits=[])
    p_vg = S.step("vector", lambda e: e.memset(vaug, 1.0), ch="dve",
                  waits=w("dve", []))

    p_evq = {}
    p_rope_last = None
    p_kv = None
    p_kelu = None
    p_vc = {}
    p_evk = {}
    p_ck = None
    for tt in range(TTN):
        tsl = slice(tt * TTS, (tt + 1) * TTS)
        for oc in range(8):
            qb = (oc % 2) * 512
            p_mm = S.step("tensor", [
                (lambda e, kc=kc, oc=oc, tsl=tsl, qb=qb: e.matmul(
                    P0[:, qb:qb + 512], wv(kc, oc * 128, oc * 128 + 128),
                    xn[:, kc, tsl],
                    start=(kc == 0), stop=(kc == 7))) for kc in range(8)],
                ch="pe", waits=w("pe", [p_evq.get(oc - 2, p_evq.get(oc + 6))]))
            p_evq[oc] = S.step("scalar", lambda e, oc=oc, qb=qb: e.copy(
                qstg[:, oc, :], P0[:, qb:qb + 512]),
                ch="act", waits=w("act", [p_mm, p_rope_last, p_kelu]))
            pump(1)
        t1q, t2q = kesc1[:, 0:512], kesc1[:, 512:1024]
        for c in range(4):
            S.step("vector", [
                (lambda e, c=c, tsl=tsl, t1q=t1q: e.tensor_mul(
                    t1q, qstg[:, c, :], cqs[:, tsl])),
                (lambda e, c=c, tsl=tsl, t2q=t2q: e.tensor_mul(
                    t2q, qstg[:, c + 4, :], sqt[:, tsl]))],
                ch="dve", waits=[p_evq[c], p_evq[c + 4], p_kv, p_tab])
            S.step("vector", lambda e, c=c, t1q=t1q, t2q=t2q: e.tensor_sub(
                qrp[:, c, :], t1q, t2q), ch="dve", waits=[])
            S.step("vector", [
                (lambda e, c=c, tsl=tsl, t1q=t1q: e.tensor_mul(
                    t1q, qstg[:, c, :], sqt[:, tsl])),
                (lambda e, c=c, tsl=tsl, t2q=t2q: e.tensor_mul(
                    t2q, qstg[:, c + 4, :], cqs[:, tsl]))],
                ch="dve", waits=[])
            p_rope_last = S.step("vector", lambda e, c=c, t1q=t1q, t2q=t2q:
                                 e.tensor_add(qrp[:, c + 4, :], t1q, t2q),
                                 ch="dve", waits=[])
        k2v = kesc2.rearrange("p (a b) -> p a b", b=512)
        kst3 = kst.rearrange("p (a b) -> p a b", b=512)
        for g in range(4):
            gs = slice(2 * g, 2 * g + 2)
            p_min = S.step("vector", lambda e, gs=gs, k2v=k2v: e.tensor_scalar_min(
                k2v, qrp[:, gs, :], 0.0), ch="dve", waits=[])
            p_exp = S.step("scalar", lambda e, k2v=k2v, kst3=kst3: e.activation(
                kst3, k2v, AF.Exp), ch="act", waits=[p_min])
            p_max = S.step("vector", lambda e, gs=gs, k2v=k2v: e.tensor_scalar_max(
                k2v, qrp[:, gs, :], 0.0), ch="dve", waits=[p_exp])
            p_rope_last = S.step("vector", lambda e, gs=gs, tsl=tsl, k2v=k2v,
                                 kst3=kst3: e.tensor_add(
                qf[:, gs, tsl], k2v, kst3), ch="dve", waits=[])
        for tb in range(4):
            tbg = tt * 4 + tb
            tbs = slice(tt * TTS + tb * 128, tt * TTS + tb * 128 + 128)
            S.dma(lambda e, tbg=tbg: e.dma_start(
                out=ckb, in_=cosk[tbg * 128:(tbg + 1) * 128, :]),
                ch="wdma", waits=[p_kelu])
            p_ck = S.dma(lambda e, tbg=tbg: e.dma_start(
                out=skb, in_=sink[tbg * 128:(tbg + 1) * 128, :]),
                ch="wdma", waits=[])
            for fc in range(2):
                kb = fc * 512
                p_mk = S.step("tensor", [
                    (lambda e, kc=kc, fc=fc, tbs=tbs, kb=kb: e.matmul(
                        P1[:, kb:kb + 512], xn[:, kc, tbs],
                        wv(kc, 1024 + fc * 512, 1536 + fc * 512),
                        start=(kc == 0), stop=(kc == 7))) for kc in range(8)],
                    ch="pe", waits=[p_evk.get((tb - 1, fc))])
                p_evk[(tb, fc)] = S.step("scalar", lambda e, fc=fc, kb=kb: e.copy(
                    kst[:, fc * 512:fc * 512 + 512], P1[:, kb:kb + 512]),
                    ch="act", waits=[p_mk, p_rope_last, p_kelu])
            p_r1 = S.step("vector", [
                lambda e: e.tensor_mul(krp[:, 0:512], kst[:, 0:512], ckb),
                lambda e: e.tensor_mul(kesc1[:, 0:512], kst[:, 512:1024], skb)],
                ch="dve", waits=[p_evk[(tb, 0)], p_evk[(tb, 1)], p_ck, p_kv])
            S.step("vector", lambda e: e.tensor_sub(krp[:, 0:512], krp[:, 0:512],
                                                    kesc1[:, 0:512]),
                   ch="dve", waits=[])
            S.step("vector", [
                lambda e: e.tensor_mul(krp[:, 512:1024], kst[:, 0:512], skb),
                lambda e: e.tensor_mul(kesc1[:, 512:1024], kst[:, 512:1024], ckb)],
                ch="dve", waits=[])
            S.step("vector", lambda e: e.tensor_add(krp[:, 512:1024],
                                                    krp[:, 512:1024],
                                                    kesc1[:, 512:1024]),
                   ch="dve", waits=[])
            qs2 = qstg[:, 0:2, :].rearrange("p a b -> p (a b)")
            p_min = S.step("vector", lambda e: e.tensor_scalar_min(
                kesc2, krp, 0.0), ch="dve", waits=[])
            p_exp = S.step("scalar", lambda e, qs2=qs2: e.activation(
                qs2, kesc2, AF.Exp), ch="act", waits=[p_min])
            p_max = S.step("vector", lambda e: e.tensor_scalar_max(
                kesc2, krp, 0.0), ch="dve", waits=[p_exp])
            p_kelu = S.step("vector", lambda e, qs2=qs2: e.tensor_add(
                kf_tb, kesc2, qs2), ch="dve", waits=[])
            for fc in range(2):
                vb = fc * 512
                p_mv = S.step("tensor", [
                    (lambda e, kc=kc, fc=fc, tbs=tbs, vb=vb: e.matmul(
                        P3[:, vb:vb + 512], xn[:, kc, tbs],
                        wv(kc, 2048 + fc * 512, 2560 + fc * 512),
                        start=(kc == 0), stop=(kc == 7))) for kc in range(8)],
                    ch="pe", waits=[p_vc.get((tb - 1, fc)), p_kv])
                p_vc[(tb, fc)] = S.step("vector", lambda e, fc=fc, vb=vb: e.tensor_copy(
                    vaug.rearrange("p (h v) -> p h v", v=65)[:, fc * 8:fc * 8 + 8, 0:64],
                    P3[:, vb:vb + 512].rearrange("p (h v) -> p h v", v=64)),
                    ch="dve", waits=[p_mv, p_vg, p_kv])
            mms = []
            order = []
            for h in range(NH):
                for eo in range(2):
                    a, b = kvslot(h, eo)
                    lead = (b == 0 and eo == 0) or (h // 4 == 3 and eo == 1)
                    order.append((0 if lead else 1, h, eo, a, b, lead))
            order.sort()
            for _, h, eo, a, b, lead in order:
                col = 32 * h + eo * 512
                mms.append(lambda e, a=a, b=b, col=col, h=h, tbg=tbg, lead=lead:
                           e.matmul(
                    P2[a:a + 32, b:b + 65], kf_tb[:, col:col + 32],
                    vaug[:, 65 * h:65 * h + 65],
                    start=(tbg == 0 and lead), stop=(tbg == 15),
                    tile_position=(0, a)))
            p_kv = S.step("tensor", mms, ch="pe",
                          waits=[p_kelu, p_vc[(tb, 0)], p_vc[(tb, 1)]])
            pump(1)
    t_ = tap("qf", [D, T], BF16)
    if t_ is not None:
        S.dma(lambda e, t_=t_: e.dma_start(
            out=t_.rearrange("(c p) t -> p c t", p=128), in_=qf[:]))
    p_b_end = S.point("pe")
    S.step("scalar", lambda e: e.copy(kv_sb[:], P2[:]))
    for h in range(NH):
        for eo in range(2):
            a, b = kvslot(h, eo)
            pc = (h // 4) + 4 * eo
            S.step("vector", lambda e, a=a, b=b, pc=pc: e.tensor_copy(
                u_sb[a:a + 32, pc:pc + 1], kv_sb[a:a + 32, b + 64:b + 65]))
    t_ = tap("kv", [128, 1024])
    if t_ is not None:
        A2 = Arena(A.t, A.cap // 1024)
        A2.reset(38 * 1024)
        kvf = A2.take([128, 1024], F32)
        S.step("vector", lambda e, kvf=kvf: e.tensor_copy(kvf, kv_sb[:]))
        S.dma(lambda e, t_=t_, kvf=kvf: e.dma_start(out=t_[:], in_=kvf))

    A.reset()
    prod = A.take([128, 8, TTS], BF16)
    indd = A.take([128, 8, 16], BF16)
    ind3s = A.take([16, 8, 128], BF16)
    dens = A.take([16, TTS], F32)
    rs = A.take([16, TTS], F32)
    rbf = A.take([16, TTS], BF16)
    rbc = A.take([128, 2, TTS], F32)
    while prep_state["i"] < len(all_jobs) + 2:
        pump(1)
    p_prep_qkv = prep_state["done_qkv"]
    t_ = tap("wb", [128, 24576], BF16)
    if t_ is not None:
        S.bg("tdbg")
        S.dma(lambda e, t_=t_: e.dma_start(out=t_[:], in_=wbig[:]), ch="tdbg",
              waits=[p_prep_qkv])
    entry = S.all_points() + [p_prep_qkv]
    started = set()

    def w(ch, pts):
        pts = [p for p in pts if p is not None]
        if ch not in started:
            started.add(ch)
            pts = pts + entry
        return pts

    S.dma(lambda e: e.dma_start(out=indd, in_=ind_den.rearrange("c p h -> p c h")),
          ch="wdma", waits=w("wdma", []))
    p_ind = S.dma(lambda e: e.dma_start(out=ind3s, in_=ind3.rearrange("c h p -> h c p")),
                  ch="wdma", waits=[])
    p_dm = p_max = p_rbf_prev = p_rbc_last = None
    p_ev = {}
    p_mult = {}
    for tt in range(TTN):
        tsl = slice(tt * TTS, (tt + 1) * TTS)
        p_pr = S.step("vector", [
            (lambda e, c=c, tsl=tsl: e.tensor_scalar_mul(
                prod[:, c, :], qf[:, c, tsl], u_sb[:, c:c + 1])) for c in range(8)],
            ch="dve", waits=w("dve", [p_dm]))
        p_dm = S.step("tensor", [
            (lambda e, c=c: e.matmul(P1[0:16, 0:512], indd[:, c, :], prod[:, c, :],
                                     start=(c == 0), stop=(c == 7))) for c in range(8)],
            ch="pe", waits=w("pe", [p_pr, p_max, p_ind]))
        p_max = S.step("vector", lambda e: e.tensor_scalar_max(
            dens, P1[0:16, 0:512], 1e-6), ch="dve", waits=[p_dm])
        p_ln = S.step("scalar", lambda e: e.activation(dens, dens, AF.Ln),
                      ch="act", waits=w("act", [p_max, p_rbf_prev]))
        p_rs = S.step("scalar", lambda e: e.activation(
            rs, dens, AF.Exp, scale=-1.0, bias=fc_sb[0:16, 2:3]),
            ch="act", waits=[p_ln])
        p_rbf = S.step("vector", lambda e: e.tensor_copy(rbf, rs),
                       ch="dve", waits=[p_rs, p_rbc_last])
        p_rbf_prev = p_rbf
        for c in range(8):
            pb = c % 2
            ab = pb * 512
            mms = []
            for j, h in enumerate((2 * c, 2 * c + 1)):
                for eo in range(2):
                    a, b = kvslot(h, eo)
                    pc = (h // 4) + 4 * eo
                    mms.append(lambda e, a=a, b=b, pc=pc, j=j, eo=eo, tsl=tsl, ab=ab:
                               e.matmul(P0[64 * j:64 * j + 64, ab:ab + 512],
                                        kv_sb[a:a + 32, b:b + 64],
                                        qf[a:a + 32, pc, tsl],
                                        start=(eo == 0), stop=(eo == 1),
                                        tile_position=(a, 64 * j)))
            p_att = S.step("tensor", mms, ch="pe", waits=[p_mult.get(c - 2)])
            p_rbc = S.step("tensor", lambda e, c=c, pb=pb: e.matmul(
                P3[:, pb * 512:pb * 512 + 512], ind3s[:, c, :], rbf,
                start=True, stop=True),
                ch="pe", waits=[p_rbf, p_ev.get(c - 2)])
            p_rbc_last = p_rbc
            p_ev[c] = S.step("scalar", lambda e, pb=pb: e.copy(
                rbc[:, pb, :], P3[:, pb * 512:pb * 512 + 512]),
                ch="act", waits=[p_rbc])
            p_mult[c] = S.step("vector", lambda e, c=c, tsl=tsl, pb=pb, ab=ab:
                               e.tensor_mul(asc[:, c, tsl], P0[:, ab:ab + 512],
                                            rbc[:, pb, :]),
                               ch="dve", waits=[p_att, p_ev[c]])
        p_mult = {-2: p_mult[6], -1: p_mult[7]}
        p_ev = {-2: p_ev[6], -1: p_ev[7]}
    t_ = tap("asc", [D, T], BF16)
    if t_ is not None:
        S.dma(lambda e, t_=t_: e.dma_start(
            out=t_.rearrange("(c p) t -> p c t", p=128), in_=asc[:]))

    Ap2 = Arena(ar_t_for_prep2 := A.t, 40)
    Ap2.reset(22528)
    p2_i = [Ap2.take([128, 1024], I8) for _ in range(2)]
    p2_f = [Ap2.take([128, 1024], F32) for _ in range(2)]
    p2c = {}
    p2i = {}
    for i in range(32):
        pb = i % 2
        if i < 24:
            jc = i
            p2i[i] = S.dma(lambda e, jc=jc, pb=pb: e.dma_start(
                out=p2_i[pb], in_=wdT[jc * 128:(jc + 1) * 128, :]),
                ch=f"pdi{pb}", waits=[p2c.get(i - 2), p_prep_qkv])
            p2c[i] = S.step("gpsimd", lambda e, jc=jc, pb=pb: e.tensor_copy(
                wdv(jc, 0, 1024), p2_i[pb]),
                ch="pcast", waits=[p2i[i], p_b_end])
        else:
            c = i - 24
            p2i[i] = S.dma(lambda e, c=c, pb=pb: e.dma_start(
                out=p2_f[pb], in_=woT[c * 128:(c + 1) * 128, :]),
                ch=f"pdi{pb}", waits=[p2c.get(i - 2)])
            p2c[i] = S.step("gpsimd", lambda e, c=c, pb=pb: e.tensor_copy(
                wo_sb[:, c, :], p2_f[pb]),
                ch="pcast", waits=[p2i[i]])
    p_prep2 = p2c[31]

    def _dstep(eng, fns, ch="main", waits="all"):
        return S.step(eng, fns, ch=ch, waits=waits)

    def _ddma(fn, ch="main", waits="all", eng="sync"):
        return S.dma(fn, ch=ch, waits=waits, eng=eng)

    A.reset()
    htile = A.take([128, 24, TTS], BF16)
    wgu = A.take([128, 2, 2048], BF16)
    wgu8 = Arena(qf, 32).take([128, 4, 2048], I8)
    mark = A.off
    sg = A.take([128, 2, TTS], F32)
    A.reset(mark)
    ts1 = A.take([128, TTS], F32)
    ts2 = A.take([128, TTS], F32)
    ts3 = A.take([128, TTS], F32)
    entry = S.all_points() + [p_prep2]
    started = set()

    def w(ch, pts):
        pts = [p for p in pts if p is not None]
        if ch not in started:
            started.add(ch)
            pts = pts + entry
        return pts

    p_pe = {}
    p_cast = {}
    p_dma = {}
    p_act_by = {0: None, 1: None}
    p_dve_by = {0: None, 1: None}
    p_oact = {}
    p_st = None
    NS = TTN * 24

    def _gu_dma(s):
        jc2 = s % 24
        q = s % 4
        p_dma[s] = _ddma(lambda e, jc=jc2, q=q: e.dma_start(
            out=wgu8[:, q, :], in_=wguT[jc]),
            ch=f"wdma{q}", eng="sync",
            waits=w(f"wdma{q}", [p_cast.get(s - 4)]))

    def _gu_cast(s):
        pb = s % 2
        q = s % 4
        p_cast[s] = _dstep("vector", lambda e, pb=pb, q=q: e.tensor_copy(
            wgu[:, pb, :], wgu8[:, q, :]), ch="dve",
            waits=w("dve", [p_dma[s], p_pe.get(s - 2)]))

    for tt in range(TTN):
        tsl = slice(tt * TTS, (tt + 1) * TTS)
        for jc in range(24):
            s = tt * 24 + jc
            pb = s % 2
            pg = P0 if pb == 0 else P2
            if s == 0:
                for k in range(3):
                    _gu_dma(k)
                _gu_cast(0)
            mmg = [(lambda e, kc=kc, tsl=tsl, pb=pb, pg=pg: e.matmul(
                pg[:, 0:512], wgu[:, pb, kc * 128:(kc + 1) * 128], xn[:, kc, tsl],
                start=(kc == 0), stop=(kc == 7))) for kc in range(8)]
            mmu = [(lambda e, kc=kc, tsl=tsl, pb=pb, pg=pg: e.matmul(
                pg[:, 512:1024], wgu[:, pb, 1024 + kc * 128:1024 + (kc + 1) * 128],
                xn[:, kc, tsl],
                start=(kc == 0), stop=(kc == 7))) for kc in range(8)]
            p_pe[s] = _dstep("tensor", mmg + mmu, ch="pe",
                             waits=w("pe", [p_cast[s], p_act_by[pb], p_dve_by[pb]]))
            if s + 3 < NS:
                _gu_dma(s + 3)
            if s + 1 < NS:
                _gu_cast(s + 1)
            p_act_by[pb] = _dstep("scalar", lambda e, pb=pb, pg=pg: e.activation(
                sg[:, pb, :], pg[:, 0:512], AF.Silu, scale=DQ),
                ch="act", waits=w("act", [p_pe[s]]))
            p_dve_by[pb] = _dstep("vector", lambda e, jc=jc, pb=pb, pg=pg:
                                  e.tensor_mul(
                htile[:, jc, :], sg[:, pb, :], pg[:, 512:1024]),
                ch="dve", waits=w("dve", [p_act_by[pb]]))
        p_dve = p_dve_by[1]
        p_act = p_act_by[1]
        for oc in range(8):
            pq = P1 if oc % 2 == 0 else P3
            mms = [
                (lambda e, jc=jc, oc=oc, pq=pq: e.matmul(
                    pq[:, 0:512], wdv(jc, oc * 128, oc * 128 + 128), htile[:, jc, :],
                    start=(jc == 0), stop=False)) for jc in range(24)]
            mms += [
                (lambda e, vc=vc, oc=oc, tsl=tsl, pq=pq: e.matmul(
                    pq[:, 0:512], wo_sb[:, vc, oc * 128:(oc + 1) * 128], asc[:, vc, tsl],
                    start=False, stop=(vc == 7))) for vc in range(8)]
            p_mm = _dstep("tensor", mms, ch="pe", waits=w("pe", [
                p_oact.get(oc % 2), p_dve if oc == 0 else None]))
            p_x = _ddma(lambda e, oc=oc, tsl=tsl: e.dma_start(
                out=ts2, in_=x.rearrange("(c p) t -> p c t", p=128)[:, oc, tsl]),
                ch="xdma", waits=w("xdma", [p_dve]))
            p_act = _dstep("scalar", lambda e, pq=pq: e.mul(ts1, pq[:, 0:512],
                                                           DQ * DQ),
                           ch="act", waits=w("act", [p_mm, p_dve]))
            p_oact[oc % 2] = p_act
            p_dve = _dstep("vector", lambda e: e.tensor_add(ts3, ts1, ts2),
                           ch="dve", waits=w("dve", [p_act, p_x, p_st]))
            p_st = _ddma(lambda e, oc=oc, tsl=tsl: e.dma_start(
                out=y.rearrange("(c p) t -> p c t", p=128)[:, oc, tsl], in_=ts3),
                ch="st", waits=w("st", [p_dve]))


def build_l0(taps=()):
    nc = bass.Bass("TRN2", target_bir_lowering=False)
    es = ExitStack()
    S = Sched(nc, es)

    x = nc.dram_tensor("x", [D, T], F32, kind="ExternalInput")
    wqkvT = nc.dram_tensor("wqkvT", [D, HID], F32, kind="ExternalInput")
    wguT = nc.dram_tensor("wguT", [24, 128, 2048], I8, kind="ExternalInput")
    wdT = nc.dram_tensor("wdT", [HID, D], I8, kind="ExternalInput")
    woT = nc.dram_tensor("woT", [D, D], F32, kind="ExternalInput")
    cosq = nc.dram_tensor("cosq", [128, T], BF16, kind="ExternalInput")
    sinq = nc.dram_tensor("sinq", [128, T], BF16, kind="ExternalInput")
    cosk = nc.dram_tensor("cosk", [T, 512], BF16, kind="ExternalInput")
    sink = nc.dram_tensor("sink", [T, 512], BF16, kind="ExternalInput")
    ind_den = nc.dram_tensor("ind_den", [8, 128, 16], BF16, kind="ExternalInput")
    ind3 = nc.dram_tensor("ind3", [8, 16, 128], BF16, kind="ExternalInput")
    fconst = nc.dram_tensor("fconst", [128, 8], F32, kind="ExternalInput")
    y = nc.dram_tensor("y", [D, T], F32, kind="ExternalOutput")

    dbg = {}

    def tap(name, shape, dt=F32):
        if name in taps:
            dbg[name] = nc.dram_tensor("dbg_" + name, shape, dt, kind="ExternalOutput")
        return dbg.get(name)

    def sbt(name, shape, dt):
        return es.enter_context(nc.sbuf_tensor(name, shape, dt))

    xn = sbt("xn", [128, 8, T], BF16)
    qf = sbt("qf", [128, 8, T], BF16)
    asc = sbt("asc", [128, 8, T], BF16)
    wbig = sbt("wbig", [128, 24576], BF16)
    wo_sb = sbt("wo_sb", [128, 8, D], BF16)
    kv_sb = sbt("kv_sb", [128, 1024], BF16)
    u_sb = sbt("u_sb", [128, 8], F32)
    ones_bf = sbt("ones_bf", [128, 128], BF16)
    ones_f = sbt("ones_f", [1, 128], F32)
    fc_sb = sbt("fc_sb", [128, 8], F32)
    ar_t = sbt("arena", [128, 11264], F32)
    A = Arena(ar_t, 44)

    P0 = es.enter_context(nc.psum_tensor("P0", [128, 1024], F32))
    P1 = es.enter_context(nc.psum_tensor("P1", [128, 1024], F32))
    P2 = es.enter_context(nc.psum_tensor("P2", [128, 1024], F32))
    P3 = es.enter_context(nc.psum_tensor("P3", [128, 1024], F32))

    emit_l0(nc, es, S, tap, A, P0, P1, P2, P3, xn, qf, asc, wbig, wo_sb, kv_sb, u_sb, ones_bf, ones_f, fc_sb, x, wqkvT, wguT, wdT, woT, cosq, sinq, cosk, sink, ind_den, ind3, fconst, y)
    S.emit()
    return nc, es


def emit_l1(nc, es, S, tap, A, P0, P1, P2, P3, xn1, q1, k1, vaug1, idb, ones_bf, ones_f, i32, i32T, ga, gb, bqs, bks, bv_bf, bsum, cneg, fc_sb, gam_s, bet_s, x, wqkv1, bqkv1, wo1, bo1, gam, bet, w1T, b1, vr, cT, indH, ind32, ind32T, ident, fconst, y, wqkv1_bf, scr2):
    def avslot(qt):
        if qt < 7:
            return P2[:, qt * 65: qt * 65 + 65]
        if qt < 14:
            return P2[:, 512 + (qt - 7) * 65: 512 + (qt - 7) * 65 + 65]
        return P3[:, (qt - 14) * 65: (qt - 14) * 65 + 65]

    S.dma(lambda e: e.dma_start(out=idb[:], in_=ident[:]))
    S.dma(lambda e: e.dma_start(out=fc_sb[:], in_=fconst[:]))
    S.dma(lambda e: e.dma_start(out=i32[:], in_=ind32[:]))
    S.dma(lambda e: e.dma_start(out=i32T[:], in_=ind32T[:]))
    S.step("vector", lambda e: e.memset(ones_bf[:], 1.0))
    S.step("vector", lambda e: e.memset(ones_f[:], 1.0))
    S.step("vector", lambda e: e.memset(vaug1[:], 1.0))

    A.reset()
    tv1 = A.take([128, 8], F32)
    tv2 = A.take([128, 8], F32)
    rowc = A.take([1, 1024], F32)
    sv_off = A.off
    S.dma(lambda e: e.dma_start(out=gam_s[:], in_=gam.rearrange("(c p) -> p c", p=128),
                                allow_slow_non_contiguous=True))
    S.dma(lambda e: e.dma_start(out=bet_s[:], in_=bet.rearrange("(c p) -> p c", p=128),
                                allow_slow_non_contiguous=True))
    S.dma(lambda e: e.dma_start(out=tv1, in_=bqkv1[0:1024].rearrange("(c p) -> p c", p=128),
                                allow_slow_non_contiguous=True))
    S.step("vector", lambda e: e.tensor_scalar_mul(bqs[:], tv1, SCALE))
    S.dma(lambda e: e.dma_start(out=bks[:], in_=bqkv1[1024:2048].rearrange("(c p) -> p c", p=128),
                                allow_slow_non_contiguous=True))
    S.dma(lambda e: e.dma_start(out=rowc, in_=bqkv1[2048:3072].rearrange("(o f) -> o f", o=1)))
    S.step("vector", lambda e: e.tensor_copy(bv_bf[:], rowc))
    S.dma(lambda e: e.dma_start(out=tv1, in_=b1.rearrange("(c p) -> p c", p=128),
                                allow_slow_non_contiguous=True))
    S.dma(lambda e: e.dma_start(out=tv2, in_=bo1.rearrange("(c p) -> p c", p=128),
                                allow_slow_non_contiguous=True))
    S.step("vector", lambda e: e.tensor_add(bsum[:], tv1, tv2))
    S.bg("pdi0")
    S.bg("pdi1")
    S.bg("pdo0")
    S.bg("pdo1")
    S.bg("pcast")
    l1_entry = S.all_points()
    Aq1 = Arena(q1, 32)
    q_f = [Aq1.take([128, 1024], F32) for _ in range(2)]
    q_b = [Aq1.take([128, 1024], BF16) for _ in range(2)]
    pq_i = {}
    pq_c = {}
    pq_o = {}
    for i in range(24):
        c, j = i // 3, i % 3
        pb = i % 2
        pq_i[i] = S.dma(lambda e, c=c, j=j, pb=pb: e.dma_start(
            out=q_f[pb], in_=wqkv1[c * 128:(c + 1) * 128, j * 1024:(j + 1) * 1024]),
            ch=f"pdi{pb}", waits=[pq_c.get(i - 2)] + (l1_entry if i < 2 else []))
        pq_c[i] = S.step("gpsimd", lambda e, pb=pb: e.tensor_copy(
            q_b[pb], q_f[pb]), ch="pcast", waits=[pq_i[i], pq_o.get(i - 2)])
        pq_o[i] = S.dma(lambda e, c=c, j=j, pb=pb: e.dma_start(
            out=wqkv1_bf[c * 128:(c + 1) * 128, j * 1024:(j + 1) * 1024], in_=q_b[pb]),
            ch=f"pdo{pb}", waits=[pq_c[i]])
    p_prep_q1 = pq_o[23]

    A.reset()
    xst = A.take([128, 2, TTS], F32)
    xbf = A.take([128, 8, TTS], BF16)
    sqs = A.take([128, 2, TTS], BF16)
    row = A.take([1, 2048], F32)
    abc = A.take([128, TTS], F32)
    bbc = A.take([128, TTS], F32)
    tmul = A.take([128, 2, TTS], BF16)
    spart = A.take([128, 8, 4], F32)
    qpart = A.take([128, 8, 4], F32)
    gtmp = A.take([4, 64], F32)
    s1r = A.take([128, 8], F32)
    q1r = A.take([128, 8], F32)
    entry = S.all_points()
    started = set()

    def w(ch, pts):
        pts = [p for p in pts if p is not None]
        if ch not in started:
            started.add(ch)
            pts = pts + entry
        return pts

    xre = x.rearrange("(c p) t -> p c t", p=128)
    p_cp = {}
    p_sq = {}
    p_ms = p_mq = p_mu = p_acc = None
    p_ap = {}
    pab = pbb = None
    for tt in range(TTN):
        tsl = slice(tt * TTS, (tt + 1) * TTS)
        for c in range(8):
            pb = c % 2
            p_x = S.dma(lambda e, c=c, tsl=tsl, pb=pb: e.dma_start(
                out=xst[:, pb, :], in_=xre[:, c, tsl]),
                ch=f"xdma{pb}", waits=w(f"xdma{pb}", [p_cp.get(c - 2), p_sq.get(c - 2)] +
                                        ([p_mu] if c < 2 else [])))
            p_cp[c] = S.step("vector", lambda e, c=c, pb=pb: e.tensor_copy(
                xbf[:, c, :], xst[:, pb, :]), ch="dve", waits=w("dve", [p_x]))
            p_sq[c] = S.step("scalar", lambda e, pb=pb: e.activation(
                sqs[:, pb, :], xst[:, pb, :], AF.Square),
                ch="act", waits=w("act", [p_x, p_mq if c >= 2 else p_acc]))
            p_ms = S.step("tensor", lambda e, c=c: e.matmul(
                P1[0:1, 0:512], ones_bf[:, 0:1], xbf[:, c, :],
                start=(c == 0), stop=(c == 7)),
                ch="pe", waits=w("pe", [p_cp[c]] + ([pab] if c == 0 else [])))
            p_mq = S.step("tensor", lambda e, c=c, pb=pb: e.matmul(
                P1[0:1, 512:1024], ones_bf[:, 0:1], sqs[:, pb, :],
                start=(c == 0), stop=(c == 7)),
                ch="pe", waits=[p_sq[c]])
        pab, pbb, _ = _ln_rows_p(S, w, row, P1[0:1, 0:512], P1[0:1, 512:1024],
                                 P0[:, 0:512], P0[:, 512:1024], abc, bbc, ones_f,
                                 fc_sb[0:1, 0:1], [p_ms, p_mq],
                                 [pab, pbb])
        for c in range(8):
            pb = c % 2
            p_x = S.dma(lambda e, c=c, tsl=tsl, pb=pb: e.dma_start(
                out=xst[:, pb, :], in_=xre[:, c, tsl]),
                ch=f"xdma{pb}", waits=[p_ap.get(c - 2), p_cp[7], p_sq[7]])
            p_mu = S.step("vector", lambda e, pb=pb: e.tensor_mul(
                tmul[:, pb, :], xst[:, pb, :], abc[:]), ch="dve",
                waits=[p_x, pab])
            p_ap[c] = S.step("vector", lambda e, c=c, tsl=tsl, pb=pb: e.tensor_add(
                xn1[:, c, tsl], tmul[:, pb, :], bbc[:]), ch="dve", waits=[pbb])
            S.step("vector", lambda e, c=c, tt=tt, tsl=tsl: e.reduce_sum(
                spart[:, c, tt:tt + 1], xn1[:, c, tsl], axis=AX.X),
                ch="dve", waits=[])
            p_acc = S.step("scalar", lambda e, c=c, tt=tt, tsl=tsl, pb=pb:
                           e.activation(sqs[:, pb, :], xn1[:, c, tsl], AF.Square,
                                        accum_out=qpart[:, c, tt:tt + 1]),
                           ch="act", waits=[p_ap[c], p_mq])
        p_cp = {}
        p_sq = {}
    for c in range(8):
        S.step("vector", lambda e, c=c: e.reduce_sum(s1r[:, c:c + 1], spart[:, c, :], axis=AX.X))
        S.step("vector", lambda e, c=c: e.reduce_sum(q1r[:, c:c + 1], qpart[:, c, :], axis=AX.X))
    S.step("tensor", lambda e: e.matmul(P1[0:4, 0:8], i32[:], s1r, start=True, stop=True))
    S.step("tensor", lambda e: e.matmul(P1[0:4, 8:16], i32[:], q1r, start=True, stop=True))
    NG = 1.0 / (T * 32.0)
    S.step("scalar", lambda e: e.mul(gtmp[0:4, 0:16], P1[0:4, 0:16], NG))
    S.step("scalar", lambda e: e.activation(gtmp[0:4, 16:24], gtmp[0:4, 0:8], AF.Square))
    S.step("vector", lambda e: e.tensor_sub(gtmp[0:4, 24:32], gtmp[0:4, 8:16],
                                            gtmp[0:4, 16:24]))
    S.step("scalar", lambda e: e.activation(gtmp[0:4, 32:40], gtmp[0:4, 24:32],
                                            AF.Ln, bias=fc_sb[0:4, 1:2]))
    S.step("scalar", lambda e: e.activation(gtmp[0:4, 40:48], gtmp[0:4, 32:40],
                                            AF.Exp, scale=-0.5))
    S.step("tensor", lambda e: e.matmul(P0[:, 0:8], i32T[:], gtmp[0:4, 0:8],
                                        start=True, stop=True))
    S.step("tensor", lambda e: e.matmul(P0[:, 8:16], i32T[:], gtmp[0:4, 40:48],
                                        start=True, stop=True))
    S.step("scalar", lambda e: e.copy(s1r, P0[:, 0:8]))
    S.step("scalar", lambda e: e.copy(q1r, P0[:, 8:16]))
    t_ = tap("gnraw", [128, 32])
    if t_ is not None:
        S.dma(lambda e, t_=t_: e.dma_start(out=t_[:, 0:8], in_=s1r))
        S.dma(lambda e, t_=t_: e.dma_start(out=t_[:, 8:16], in_=q1r))
        S.dma(lambda e, t_=t_: e.dma_start(out=t_[:, 16:24], in_=gam_s[:]))
        S.dma(lambda e, t_=t_: e.dma_start(out=t_[:, 24:32], in_=bet_s[:]))
    t_ = tap("gtmp", [4, 64])
    if t_ is not None:
        S.dma(lambda e, t_=t_: e.dma_start(out=t_[:], in_=gtmp))
    S.step("vector", lambda e: e.tensor_mul(ga[:], q1r, gam_s[:]))
    S.step("vector", lambda e: e.tensor_mul(gb[:], s1r, ga[:]))
    S.step("vector", lambda e: e.tensor_sub(gb[:], bet_s[:], gb[:]))
    t_ = tap("xn1", [D, T], BF16)
    if t_ is not None:
        S.dma(lambda e, t_=t_: e.dma_start(
            out=t_.rearrange("(c p) t -> p c t", p=128), in_=xn1[:]))

    A.reset()
    nq = A.take([16, T], F32)
    nk = A.take([16, T], F32)
    m_end = A.off
    A.reset(m_end)
    hg = A.take([128, 8, TTS], BF16)
    sqs2 = A.take([128, 8, TTS], BF16)
    wpcq = A.take([128, 2, 8, 128], BF16)
    wpck = A.take([128, 2, 8, 128], BF16)
    wpv = A.take([128, 2, 8, 512], BF16)
    scr2.reset()
    mrow = scr2.take([16, T], F32)
    indHs = scr2.take([128, 8, 16], BF16)
    cval = scr2.take([16, 4], F32)
    cvalb = scr2.take([16, 1], BF16)
    crow = scr2.take([1, 16], F32)
    entry = S.all_points()
    started = set()

    def w(ch, pts):
        pts = [p for p in pts if p is not None]
        if ch not in started:
            started.add(ch)
            pts = pts + entry
        return pts

    entry = entry + [p_prep_q1]
    wre = wqkv1_bf.rearrange("(kc p) j -> p kc j", p=128)
    S.dma(lambda e: e.dma_start(out=indHs, in_=indH.rearrange("c p h -> p c h")),
          ch="wdma", waits=w("wdma", []))
    p_wv = None
    for fc in range(2):
        p_wv = S.dma(lambda e, fc=fc: e.dma_start(
            out=wpv[:, fc], in_=wre[:, :, 2048 + fc * 512:2048 + (fc + 1) * 512]),
            ch="wdma", waits=[])
    p_v_last = None
    p_vc = None
    p_vc2 = None
    p_qe = {}
    p_ke = {}
    p_q = {}
    p_k = {}
    p_nke = None
    p_nkm = None
    for tt in range(TTN):
        tsl = slice(tt * TTS, (tt + 1) * TTS)
        p_hg = S.step("vector", [
            (lambda e, c=c, tsl=tsl: e.tensor_scalar(
                hg[:, c, :], xn1[:, c, tsl], ga[:, c:c + 1], gb[:, c:c + 1],
                ALU.mult, ALU.add)) for c in range(8)],
            ch="dve", waits=w("dve", [p_v_last]))
        for oc in range(8):
            pb = oc % 2
            p_wq = S.dma(lambda e, oc=oc, pb=pb: e.dma_start(
                out=wpcq[:, pb], in_=wre[:, :, oc * 128:(oc + 1) * 128]),
                ch=f"fwq{pb}", waits=w(f"fwq{pb}", [p_q.get(oc - 2, p_q.get(oc + 6))]))
            p_wk = S.dma(lambda e, oc=oc, pb=pb: e.dma_start(
                out=wpck[:, pb], in_=wre[:, :, 1024 + oc * 128:1024 + (oc + 1) * 128]),
                ch=f"fwk{pb}", waits=w(f"fwk{pb}", [p_k.get(oc - 2, p_k.get(oc + 6))]))
            p_q[oc] = S.step("tensor", [
                (lambda e, kc=kc, pb=pb: e.matmul(
                    P0[:, 0:512], wpcq[:, pb, kc, :], hg[:, kc, :],
                    start=(kc == 0), stop=(kc == 7))) for kc in range(8)],
                ch="pe", waits=w("pe", [p_wq, p_hg, p_nke,
                                        p_qe.get(oc - 1, p_qe.get(oc + 7))]))
            p_qe[oc] = S.step("scalar", lambda e, oc=oc, tsl=tsl: e.activation(
                q1[:, oc, tsl], P0[:, 0:512], AF.Identity, scale=SCALE,
                bias=bqs[:, oc:oc + 1]),
                ch="act", waits=w("act", [p_q[oc]]))
            p_k[oc] = S.step("tensor", [
                (lambda e, kc=kc, pb=pb: e.matmul(
                    P0[:, 512:1024], wpck[:, pb, kc, :], hg[:, kc, :],
                    start=(kc == 0), stop=(kc == 7))) for kc in range(8)],
                ch="pe", waits=[p_wk, p_ke.get(oc - 1, p_ke.get(oc + 7))])
            p_ke[oc] = S.step("scalar", lambda e, oc=oc, tsl=tsl: e.activation(
                k1[:, oc, tsl], P0[:, 512:1024], AF.Identity,
                bias=bks[:, oc:oc + 1]),
                ch="act", waits=[p_k[oc]])
        for fc in range(2):
            for tb in range(4):
                tbg = tt * 4 + tb
                vb = (fc * 4 + tb) % 2
                mms = [
                    (lambda e, kc=kc, tb=tb, fc=fc, vb=vb: e.matmul(
                        P1[:, vb * 512:vb * 512 + 512],
                        hg[:, kc, tb * 128:(tb + 1) * 128], wpv[:, fc, kc, :],
                        start=(kc == 0), stop=False)) for kc in range(8)]
                mms.append(lambda e, fc=fc, vb=vb: e.matmul(
                    P1[:, vb * 512:vb * 512 + 512], ones_bf[0:1, 0:128],
                    bv_bf[0:1, fc * 512:(fc + 1) * 512],
                    start=False, stop=True))
                p_v = S.step("tensor", mms, ch="pe", waits=[p_vc2, p_nkm, p_wv])
                p_vc2 = p_vc
                p_vc = S.step("vector", lambda e, fc=fc, tbg=tbg, vb=vb: e.tensor_copy(
                    vaug1[:, tbg, fc * 520:fc * 520 + 520].rearrange(
                        "p (h v) -> p h v", v=65)[:, :, 0:64],
                    P1[:, vb * 512:vb * 512 + 512].rearrange("p (h v) -> p h v", v=64)),
                    ch="dve", waits=[p_v])
        p_v_last = p_v
        p_sqq = S.step("scalar", [
            (lambda e, c=c, tsl=tsl: e.activation(
                sqs2[:, c, :], q1[:, c, tsl], AF.Square)) for c in range(8)],
            ch="act", waits=[p_qe[7], p_nkm])
        p_nqm = S.step("tensor", [
            (lambda e, c=c: e.matmul(P0[0:16, 0:512], indHs[:, c, :], sqs2[:, c, :],
                                     start=(c == 0), stop=(c == 7))) for c in range(8)],
            ch="pe", waits=[p_sqq, p_qe[7], p_wv])
        p_nqe = S.step("scalar", lambda e, tsl=tsl: e.copy(nq[:, tsl], P0[0:16, 0:512]),
                       ch="act", waits=[p_nqm])
        p_sqk = S.step("scalar", [
            (lambda e, c=c, tsl=tsl: e.activation(
                sqs2[:, c, :], k1[:, c, tsl], AF.Square)) for c in range(8)],
            ch="act", waits=[p_nqm, p_ke[7]])
        p_nkm = S.step("tensor", [
            (lambda e, c=c: e.matmul(P0[0:16, 0:512], indHs[:, c, :], sqs2[:, c, :],
                                     start=(c == 0), stop=(c == 7))) for c in range(8)],
            ch="pe", waits=[p_sqk, p_nqe])
        p_nke = S.step("scalar", lambda e, tsl=tsl: e.copy(nk[:, tsl], P0[0:16, 0:512]),
                       ch="act", waits=[p_nkm])
    S.step("vector", lambda e: e.reduce_max(cval[:, 0:1], nk, axis=AX.X))
    S.step("scalar", lambda e: e.activation(cval[:, 1:2], cval[:, 0:1], AF.Ln))
    S.step("vector", lambda e: e.tensor_scalar_mul(cval[:, 2:3], cval[:, 1:2], 0.5))
    S.step("scalar", lambda e: e.activation(mrow, nq, AF.Ln))
    S.step("scalar", lambda e: e.activation(mrow, mrow, AF.Exp, scale=0.5,
                                            bias=cval[:, 2:3]))
    for qh in range(2):
        S.step("vector", lambda e, qh=qh: e.tensor_reduce(
            cval[:, 3:4], mrow[:, qh * 1024:(qh + 1) * 1024], AX.X, ALU.max,
            negate=True))
        S.step("vector", lambda e: e.tensor_copy(cvalb, cval[:, 3:4]))
        S.step("tensor", lambda e: e.transpose(
            P1[:].bitcast(BF16)[0:1, 0:16], cvalb, idb[0:16, 0:16]))
        S.step("scalar", lambda e: e.copy(crow, P1[:].bitcast(BF16)[0:1, 0:16]))
        S.step("tensor", lambda e: e.matmul(P0[:, 0:16], ones_f[0:1, 0:128], crow,
                                            start=True, stop=True))
        S.step("scalar", lambda e, qh=qh: e.copy(cneg[:, qh * 16:qh * 16 + 16], P0[:, 0:16]))

    t_ = tap("q1", [D, T], BF16)
    if t_ is not None:
        S.dma(lambda e, t_=t_: e.dma_start(
            out=t_.rearrange("(c p) t -> p c t", p=128), in_=q1[:]))
    t_ = tap("hgw", [D, HID], BF16)
    if t_ is not None:
        S.dma(lambda e, t_=t_: e.dma_start(out=t_[:], in_=wqkv1_bf[:]))
    t_ = tap("nq", [16, T])
    if t_ is not None:
        S.dma(lambda e, t_=t_: e.dma_start(out=t_[:], in_=nq))
    t_ = tap("mrow", [16, T])
    if t_ is not None:
        S.dma(lambda e, t_=t_: e.dma_start(out=t_[:], in_=mrow))
    t_ = tap("cneg", [128, 32])
    if t_ is not None:
        S.dma(lambda e, t_=t_: e.dma_start(out=t_[:], in_=cneg[:]))

    A.reset()
    pbf = A.take([128, 2, 1024], BF16)
    srow = A.take([128, 16], F32)
    rrow = A.take([128, 16], F32)
    abuf = A.take([128, 16, 128], BF16)
    qz = A.take([128, 2, 2048], BF16)
    entry = S.all_points()
    started = set()

    def w(ch, pts):
        pts = [p for p in pts if p is not None]
        if ch not in started:
            started.add(ch)
            pts = pts + entry
        return pts

    p_z = S.step("vector", [
        lambda e: e.memset(qz[64:128, 0, :], 0.0),
        lambda e: e.memset(qz[0:64, 1, :], 0.0)], ch="dve", waits=w("dve", []))

    def qk_mm(h, kc, qh):
        hp = h // 2
        p_s = P0 if qh == 0 else P1
        return [(lambda e, j=j, kc=kc, qh=qh, hp=hp, h=h, p_s=p_s: e.matmul(
            p_s[:, j * 512:(j + 1) * 512],
            k1[:, hp, kc * 128:(kc + 1) * 128],
            qz[:, h % 2, qh * 1024 + j * 512:qh * 1024 + (j + 1) * 512],
            start=True, stop=True)) for j in range(2)]

    def av_mm(h, kc, qh):
        return [(lambda e, qc=qc, kc=kc, qh=qh, h=h: e.matmul(
            avslot(qh * 8 + qc), pbf[:, qh, qc * 128:(qc + 1) * 128],
            vaug1[:, kc, 65 * h:65 * h + 65],
            start=(kc == 0 and (qh * 8 + qc) in (0, 7, 14)),
            stop=(kc == 15))) for qc in range(8)]

    def fill_mm(n):
        return [(lambda e: e.matmul(
            P0[:, 0:512], ones_bf[:, 0:128], pbf[:, 0, 0:512],
            start=True, stop=True)) for _ in range(n)]

    s_war = {0: None, 1: None}
    p_div = None
    p_tdma = None
    p_qz = {}
    p_qk_last = {}
    for hp in range(8):
        for h2 in range(2):
            h = 2 * hp + h2
            hb = 64 * h2
            p_qz[h2] = S.step("vector", lambda e, hb=hb, hp=hp, h2=h2:
                              e.tensor_copy(qz[hb:hb + 64, h2, :],
                                            q1[hb:hb + 64, hp, :]),
                              ch="dve", waits=[p_qk_last.get(h2)])
            for kc in range(16):
                for qh in range(2):
                    ops = fill_mm(2) if kc == 0 and qh == 0 else []
                    ops += (av_mm(h, kc - 1, qh) if kc > 0 else []) + qk_mm(h, kc, qh)
                    pw = [s_war[qh]]
                    if kc == 0:
                        pw.append(p_qz[h2])
                    if kc == 1:
                        pw.append(p_div)
                    p_pe = S.step("tensor", ops, ch="pe", waits=w("pe", pw))
                    s_war[qh] = S.step("scalar", lambda e, qh=qh, h=h: e.activation(
                        pbf[:, qh, :], (P0 if qh == 0 else P1)[:], AF.Exp,
                        bias=cneg[:, qh * 16 + h:qh * 16 + h + 1]),
                        ch="act", waits=w("act", [p_pe]))
            p_qk_last[h2] = p_pe
            p_pe = S.step("tensor", av_mm(h, 15, 0) + av_mm(h, 15, 1),
                          ch="pe", waits=[s_war[0], s_war[1]])
            p_sr = S.step("vector", [
                lambda e: e.tensor_copy(
                    srow[:, 0:7],
                    P2[:, 0:455].rearrange("p (a v) -> p a v", v=65)[:, :, 64:65].squeeze(2)),
                lambda e: e.tensor_copy(
                    srow[:, 7:14],
                    P2[:, 512:967].rearrange("p (a v) -> p a v", v=65)[:, :, 64:65].squeeze(2)),
                lambda e: e.tensor_copy(
                    srow[:, 14:16],
                    P3[:, 0:130].rearrange("p (a v) -> p a v", v=65)[:, :, 64:65].squeeze(2)),
            ], ch="dve", waits=w("dve", [p_pe]))
            p_a = S.step("scalar", lambda e: e.activation(srow[:], srow[:], AF.Ln),
                         ch="act", waits=[p_sr])
            p_a = S.step("scalar", lambda e: e.activation(rrow[:], srow[:], AF.Exp,
                                                          scale=-1.0),
                         ch="act", waits=[p_a])
            hb = 64 * h2
            p_div = S.step("vector", [
                (lambda e, qt=qt, hb=hb: e.tensor_scalar_mul(
                    abuf[:, qt, hb:hb + 64], avslot(qt)[:, 0:64],
                    rrow[:, qt:qt + 1])) for qt in range(16)],
                ch="dve", waits=[p_a] + ([p_tdma] if h2 == 0 else []))
        for qt in range(16):
            p_tdma = S.dma(lambda e, qt=qt, hp=hp: e.dma_start_transpose(
                k1[:, hp, qt * 128:(qt + 1) * 128], abuf[:, qt, :]),
                ch="tdma", waits=(w("tdma", [p_div]) if qt == 0 else []))
    t_ = tap("atm", [D, T], BF16)
    if t_ is not None:
        S.dma(lambda e, t_=t_: e.dma_start(
            out=t_.rearrange("(c p) t -> p c t", p=128), in_=k1[:]))

    A.reset()
    w1s = A.take([128, 8, D], BF16)
    wo1s = A.take([128, 8, D], BF16)
    cts = A.take([32, D], BF16)
    vrs = A.take([128, 256], BF16)
    t1b = A.take([32, T], BF16)
    ts1 = A.take([128, TTS], F32)
    ts2 = A.take([128, TTS], F32)
    ts3 = A.take([128, TTS], F32)
    wst2 = A.take([128, 2, 1024], F32)
    entry = S.all_points()
    started = set()

    def w(ch, pts):
        pts = [p for p in pts if p is not None]
        if ch not in started:
            started.add(ch)
            pts = pts + entry
        return pts

    p_cp = {}
    loads = [(w1T, w1s, c) for c in range(8)] + [(wo1, wo1s, c) for c in range(8)]
    for i, (srcb, dstb, c) in enumerate(loads):
        pb = i % 2
        p_l = S.dma(lambda e, srcb=srcb, c=c, pb=pb: e.dma_start(
            out=wst2[:, pb, :], in_=srcb[c * 128:(c + 1) * 128, :]),
            ch=f"hwl{pb}", waits=w(f"hwl{pb}", [p_cp.get(i - 2)]))
        p_cp[i] = S.step("vector", lambda e, dstb=dstb, c=c, pb=pb: e.tensor_copy(
            dstb[:, c, :], wst2[:, pb, :]), ch="dve", waits=w("dve", [p_l]))
    p_l = S.dma(lambda e: e.dma_start(out=wst2[0:32, 0, :], in_=cT[:]),
                ch="wdma", waits=[p_cp[14]])
    p_ct = S.step("vector", lambda e: e.tensor_copy(cts[:], wst2[0:32, 0, :]),
                  ch="dve", waits=[p_l])
    p_l = S.dma(lambda e: e.dma_start(
        out=wst2[:, 1, 0:256].rearrange("p (c r) -> p c r", r=32),
        in_=vr.rearrange("(c p) r -> p c r", p=128)),
        ch="wdma", waits=[p_cp[15]])
    p_vr = S.step("vector", lambda e: e.tensor_copy(vrs[:], wst2[:, 1, 0:256]),
                  ch="dve", waits=[p_l])
    p_t1e = None
    for tt in range(TTN):
        tsl = slice(tt * TTS, (tt + 1) * TTS)
        tb1 = (tt % 2) * 512
        p_t1 = S.step("tensor", [
            (lambda e, kc=kc, tsl=tsl, tb1=tb1: e.matmul(
                P1[0:32, tb1:tb1 + 512], vrs[:, kc * 32:(kc + 1) * 32],
                xn1[:, kc, tsl],
                start=(kc == 0), stop=(kc == 7))) for kc in range(8)],
            ch="pe", waits=w("pe", [p_vr, p_t1e]))
        p_t1e = S.step("scalar", lambda e, tsl=tsl, tb1=tb1: e.copy(
            t1b[:, tsl], P1[0:32, tb1:tb1 + 512]),
            ch="act", waits=w("act", [p_t1]))
    p_evac = {}
    p_y = None
    p_add1 = None
    i = 0
    for tt in range(TTN):
        tsl = slice(tt * TTS, (tt + 1) * TTS)
        for oc in range(8):
            eb = (i % 2) * 512
            mms = [
                (lambda e, kc=kc, oc=oc, tsl=tsl, eb=eb: e.matmul(
                    P2[:, eb:eb + 512], w1s[:, kc, oc * 128:(oc + 1) * 128],
                    xn1[:, kc, tsl],
                    start=(kc == 0), stop=False)) for kc in range(8)]
            mms.append(lambda e, oc=oc, tsl=tsl, eb=eb: e.matmul(
                P2[:, eb:eb + 512], cts[0:32, oc * 128:(oc + 1) * 128],
                t1b[0:32, tsl],
                start=False, stop=False))
            mms += [
                (lambda e, vc=vc, oc=oc, tsl=tsl, eb=eb: e.matmul(
                    P2[:, eb:eb + 512], wo1s[:, vc, oc * 128:(oc + 1) * 128],
                    k1[:, vc, tsl],
                    start=False, stop=(vc == 7))) for vc in range(8)]
            p_mm = S.step("tensor", mms, ch="pe",
                          waits=[p_evac.get(i - 2), p_t1e, p_ct])
            p_x = S.dma(lambda e, oc=oc, tsl=tsl: e.dma_start(
                out=ts2, in_=x.rearrange("(c p) t -> p c t", p=128)[:, oc, tsl]),
                ch="xdma", waits=w("xdma", [p_add1]))
            p_evac[i] = S.step("scalar", lambda e, oc=oc, eb=eb: e.activation(
                ts1, P2[:, eb:eb + 512], AF.Identity, bias=bsum[:, oc:oc + 1]),
                ch="act", waits=[p_mm, p_y])
            p_add1 = S.step("vector", lambda e: e.tensor_add(ts3, ts1, ts2),
                            ch="dve", waits=[p_evac[i], p_x])
            p_add2 = S.step("vector", lambda e, oc=oc, tsl=tsl: e.tensor_add(
                ts1, ts3, xn1[:, oc, tsl]), ch="dve", waits=[])
            p_y = S.dma(lambda e, oc=oc, tsl=tsl: e.dma_start(
                out=y.rearrange("(c p) t -> p c t", p=128)[:, oc, tsl], in_=ts1),
                ch="st", waits=w("st", [p_add2]))
            i += 1


def build_l1(taps=()):
    nc = bass.Bass("TRN2", target_bir_lowering=False)
    es = ExitStack()
    S = Sched(nc, es)

    x = nc.dram_tensor("x", [D, T], F32, kind="ExternalInput")
    wqkv1 = nc.dram_tensor("wqkv1", [D, HID], F32, kind="ExternalInput")
    bqkv1 = nc.dram_tensor("bqkv1", [HID], F32, kind="ExternalInput")
    wo1 = nc.dram_tensor("wo1", [D, D], F32, kind="ExternalInput")
    bo1 = nc.dram_tensor("bo1", [D], F32, kind="ExternalInput")
    gam = nc.dram_tensor("gam", [D], F32, kind="ExternalInput")
    bet = nc.dram_tensor("bet", [D], F32, kind="ExternalInput")
    w1T = nc.dram_tensor("w1T", [D, D], F32, kind="ExternalInput")
    b1 = nc.dram_tensor("b1", [D], F32, kind="ExternalInput")
    vr = nc.dram_tensor("vr", [D, 32], F32, kind="ExternalInput")
    cT = nc.dram_tensor("cT", [32, D], F32, kind="ExternalInput")
    indH = nc.dram_tensor("indH", [8, 128, 16], BF16, kind="ExternalInput")
    ind32 = nc.dram_tensor("ind32", [128, 4], F32, kind="ExternalInput")
    ind32T = nc.dram_tensor("ind32T", [4, 128], F32, kind="ExternalInput")
    ident = nc.dram_tensor("ident", [128, 128], BF16, kind="ExternalInput")
    fconst = nc.dram_tensor("fconst", [128, 8], F32, kind="ExternalInput")
    y = nc.dram_tensor("y", [D, T], F32, kind="ExternalOutput")
    wqkv1_bf = nc.dram_tensor("wqkv1_bf", [D, HID], BF16)

    dbg = {}

    def tap(name, shape, dt=F32):
        if name in taps:
            dbg[name] = nc.dram_tensor("dbg_" + name, shape, dt, kind="ExternalOutput")
        return dbg.get(name)

    def sbt(name, shape, dt):
        return es.enter_context(nc.sbuf_tensor(name, shape, dt))

    xn1 = sbt("xn1", [128, 8, T], BF16)
    q1 = sbt("q1", [128, 8, T], BF16)
    k1 = sbt("k1", [128, 8, T], BF16)
    vaug1 = sbt("vaug1", [128, 16, 1040], BF16)
    idb = sbt("idb", [128, 128], BF16)
    ones_bf = sbt("ones_bf", [128, 128], BF16)
    ones_f = sbt("ones_f", [1, 128], F32)
    i32 = sbt("i32", [128, 4], F32)
    i32T = sbt("i32T", [4, 128], F32)
    ga = sbt("ga", [128, 8], F32)
    gb = sbt("gb", [128, 8], F32)
    bqs = sbt("bqs", [128, 8], F32)
    bks = sbt("bks", [128, 8], F32)
    bv_bf = sbt("bv_bf", [1, 1024], BF16)
    bsum = sbt("bsum", [128, 8], F32)
    cneg = sbt("cneg", [128, 32], F32)
    fc_sb = sbt("fc_sb", [128, 8], F32)
    gam_s = sbt("gam_s", [128, 8], F32)
    bet_s = sbt("bet_s", [128, 8], F32)
    ar_t = sbt("arena", [128, 14336], F32)
    A = Arena(ar_t, 56)
    spare = sbt("spare", [128, 7936], BF16)
    scr2 = Arena(spare, 15)

    P0 = es.enter_context(nc.psum_tensor("P0", [128, 1024], F32))
    P1 = es.enter_context(nc.psum_tensor("P1", [128, 1024], F32))
    P2 = es.enter_context(nc.psum_tensor("P2", [128, 1024], F32))
    P3 = es.enter_context(nc.psum_tensor("P3", [128, 1024], F32))

    emit_l1(nc, es, S, tap, A, P0, P1, P2, P3, xn1, q1, k1, vaug1, idb, ones_bf, ones_f, i32, i32T, ga, gb, bqs, bks, bv_bf, bsum, cneg, fc_sb, gam_s, bet_s, x, wqkv1, bqkv1, wo1, bo1, gam, bet, w1T, b1, vr, cT, indH, ind32, ind32T, ident, fconst, y, wqkv1_bf, scr2)
    S.emit()
    return nc, es




def build_full(taps=()):
    nc = bass.Bass("TRN2", target_bir_lowering=False)
    es = ExitStack()
    S = Sched(nc, es)

    x = nc.dram_tensor("x", [D, T], F32, kind="ExternalInput")
    wqkvT = nc.dram_tensor("wqkvT", [D, HID], F32, kind="ExternalInput")
    wguT = nc.dram_tensor("wguT", [24, 128, 2048], I8, kind="ExternalInput")
    wdT = nc.dram_tensor("wdT", [HID, D], I8, kind="ExternalInput")
    woT = nc.dram_tensor("woT", [D, D], F32, kind="ExternalInput")
    cosq = nc.dram_tensor("cosq", [128, T], BF16, kind="ExternalInput")
    sinq = nc.dram_tensor("sinq", [128, T], BF16, kind="ExternalInput")
    cosk = nc.dram_tensor("cosk", [T, 512], BF16, kind="ExternalInput")
    sink = nc.dram_tensor("sink", [T, 512], BF16, kind="ExternalInput")
    ind_den = nc.dram_tensor("ind_den", [8, 128, 16], BF16, kind="ExternalInput")
    ind3 = nc.dram_tensor("ind3", [8, 16, 128], BF16, kind="ExternalInput")
    fconst = nc.dram_tensor("fconst", [128, 8], F32, kind="ExternalInput")
    wqkv1 = nc.dram_tensor("wqkv1", [D, HID], F32, kind="ExternalInput")
    bqkv1 = nc.dram_tensor("bqkv1", [HID], F32, kind="ExternalInput")
    wo1 = nc.dram_tensor("wo1", [D, D], F32, kind="ExternalInput")
    bo1 = nc.dram_tensor("bo1", [D], F32, kind="ExternalInput")
    gam = nc.dram_tensor("gam", [D], F32, kind="ExternalInput")
    bet = nc.dram_tensor("bet", [D], F32, kind="ExternalInput")
    w1T = nc.dram_tensor("w1T", [D, D], F32, kind="ExternalInput")
    b1 = nc.dram_tensor("b1", [D], F32, kind="ExternalInput")
    vr = nc.dram_tensor("vr", [D, 32], F32, kind="ExternalInput")
    cT = nc.dram_tensor("cT", [32, D], F32, kind="ExternalInput")
    indH = nc.dram_tensor("indH", [8, 128, 16], BF16, kind="ExternalInput")
    ind32 = nc.dram_tensor("ind32", [128, 4], F32, kind="ExternalInput")
    ind32T = nc.dram_tensor("ind32T", [4, 128], F32, kind="ExternalInput")
    ident = nc.dram_tensor("ident", [128, 128], BF16, kind="ExternalInput")
    y = nc.dram_tensor("y", [D, T], F32, kind="ExternalOutput")
    xmid = nc.dram_tensor("xmid", [D, T], F32)
    wqkv1_bf = nc.dram_tensor("wqkv1_bf", [D, HID], BF16)

    dbg = {}

    def tap(name, shape, dt=F32):
        if name in taps:
            dbg[name] = nc.dram_tensor("dbg_" + name, shape, dt, kind="ExternalOutput")
        return dbg.get(name)

    def sbt(name, shape, dt):
        return es.enter_context(nc.sbuf_tensor(name, shape, dt))

    big0 = sbt("big0", [128, 8, T], BF16)
    big1 = sbt("big1", [128, 8, T], BF16)
    big2 = sbt("big2", [128, 8, T], BF16)
    wbig = sbt("wbig", [128, 24576], BF16)
    ar_t = sbt("arena", [128, 14336], F32)
    ones_bf = sbt("ones_bf", [128, 128], BF16)
    ones_f = sbt("ones_f", [1, 128], F32)
    fc_sb = sbt("fc_sb", [128, 8], F32)
    kv_sb = sbt("kv_sb", [128, 1024], BF16)
    u_sb = sbt("u_sb", [128, 8], F32)
    idb = sbt("idb", [128, 128], BF16)
    i32 = sbt("i32", [128, 4], F32)
    i32T = sbt("i32T", [4, 128], F32)
    ga = sbt("ga", [128, 8], F32)
    gb = sbt("gb", [128, 8], F32)
    bqs = sbt("bqs", [128, 8], F32)
    bks = sbt("bks", [128, 8], F32)
    bv_bf = sbt("bv_bf", [1, 1024], BF16)
    bsum = sbt("bsum", [128, 8], F32)
    cneg = sbt("cneg", [128, 32], F32)
    gam_s = sbt("gam_s", [128, 8], F32)
    bet_s = sbt("bet_s", [128, 8], F32)

    P0 = es.enter_context(nc.psum_tensor("P0", [128, 1024], F32))
    P1 = es.enter_context(nc.psum_tensor("P1", [128, 1024], F32))
    P2 = es.enter_context(nc.psum_tensor("P2", [128, 1024], F32))
    P3 = es.enter_context(nc.psum_tensor("P3", [128, 1024], F32))

    A0 = Arena(ar_t, 40)
    woA = Arena(ar_t, 56)
    woA.reset(40 * 1024)
    wo_sb = woA.take([128, 8, 1024], BF16)
    emit_l0(nc, es, S, tap, A0, P0, P1, P2, P3, big0, big1, big2, wbig, wo_sb,
            kv_sb, u_sb, ones_bf, ones_f, fc_sb, x, wqkvT, wguT, wdT, woT,
            cosq, sinq, cosk, sink, ind_den, ind3, fconst, xmid)

    A1 = Arena(ar_t, 56)
    vaug1 = wbig[:, 0:16640].rearrange("p (a b) -> p a b", b=1040)
    scr2 = Arena(wbig[:, 16640:24576], 15)
    emit_l1(nc, es, S, tap, A1, P0, P1, P2, P3, big0, big1, big2, vaug1, idb,
            ones_bf, ones_f, i32, i32T, ga, gb, bqs, bks, bv_bf, bsum, cneg,
            fc_sb, gam_s, bet_s, xmid, wqkv1, bqkv1, wo1, bo1, gam, bet, w1T,
            b1, vr, cT, indH, ind32, ind32T, ident, fconst, y, wqkv1_bf, scr2)

    S.emit()
    return nc, es



def _host_consts():
    c = {}
    inv = 10000.0 ** (-np.arange(32, dtype=np.float64) / 32.0)
    ang = np.outer(np.arange(T, dtype=np.float64), inv)
    cos, sin = np.cos(ang), np.sin(ang)
    pp = np.arange(128) % 32
    c["cosq"] = np.ascontiguousarray((cos[:, pp].T * SCALE)).astype(BF)
    c["sinq"] = np.ascontiguousarray((sin[:, pp].T * SCALE)).astype(BF)
    jj = np.arange(512) % 32
    c["cosk"] = np.ascontiguousarray(cos[:, jj]).astype(BF)
    c["sink"] = np.ascontiguousarray(sin[:, jj]).astype(BF)
    ind_den = np.zeros((8, 128, 16), np.float32)
    for ch in range(8):
        for p in range(128):
            r = ch * 128 + p
            hh = r // 32 if r < 512 else (r - 512) // 32
            ind_den[ch, p, hh] = 1.0
    c["ind_den"] = ind_den.astype(BF)
    ind3 = np.zeros((8, 16, 128), np.float32)
    for ch in range(8):
        ind3[ch, 2 * ch, 0:64] = 1.0
        ind3[ch, 2 * ch + 1, 64:128] = 1.0
    c["ind3"] = ind3.astype(BF)
    c["indH"] = np.ascontiguousarray(ind3.transpose(0, 2, 1)).astype(BF)
    g32 = np.zeros((128, 4), np.float32)
    g32[np.arange(128), np.arange(128) // 32] = 1.0
    c["ind32"] = g32
    c["ind32T"] = np.ascontiguousarray(g32.T)
    c["ident"] = np.eye(128, dtype=np.float32).astype(BF)
    fc = np.zeros((128, 8), np.float32)
    fc[:, 0] = 1e-5
    fc[:, 1] = 1e-6
    fc[:, 2] = LN127SQ
    c["fconst"] = fc
    return c


def _prep_weights(inp):
    w = {}
    perm = np.concatenate([
        (np.arange(16)[:, None] * 64 + np.arange(32)[None, :]).reshape(-1),
        (np.arange(16)[:, None] * 64 + 32 + np.arange(32)[None, :]).reshape(-1)])
    wq = np.asarray(inp["L0_attn_w_qkv"], dtype=np.float32)
    wqkv = np.concatenate([wq[0:1024][perm], wq[1024:2048][perm], wq[2048:3072]], 0)
    w["wqkvT"] = np.ascontiguousarray(wqkv.T)
    def tile_ju(m):
        return np.ascontiguousarray(
            m.T.reshape(8, 128, 24, 128).transpose(2, 1, 0, 3))
    g8 = tile_ju(np.asarray(inp["L0_ffn_w_gate"]).astype(np.int8)).reshape(24, 128, 1024)
    u8 = tile_ju(np.asarray(inp["L0_ffn_w_up"]).astype(np.int8)).reshape(24, 128, 1024)
    w["wguT"] = np.ascontiguousarray(np.concatenate([g8, u8], axis=2))
    w["wdT"] = np.ascontiguousarray(np.asarray(inp["L0_ffn_w_down"]).astype(np.int8).T)
    w["woT"] = np.ascontiguousarray(np.asarray(inp["L0_attn_w_out"], dtype=np.float32).T)
    w["wqkv1"] = np.ascontiguousarray(np.asarray(inp["L1_attn_W_qkv"], dtype=np.float32))
    w["bqkv1"] = np.ascontiguousarray(np.asarray(inp["L1_attn_b_qkv"], dtype=np.float32))
    w["wo1"] = np.ascontiguousarray(np.asarray(inp["L1_attn_W_o"], dtype=np.float32))
    w["bo1"] = np.ascontiguousarray(np.asarray(inp["L1_attn_b_o"], dtype=np.float32))
    w["gam"] = np.ascontiguousarray(np.asarray(inp["L1_attn_gn_gamma"], dtype=np.float32))
    w["bet"] = np.ascontiguousarray(np.asarray(inp["L1_attn_gn_beta"], dtype=np.float32))
    w["w1T"] = np.ascontiguousarray(np.asarray(inp["L1_ffn_W"], dtype=np.float32).T)
    w["b1"] = np.ascontiguousarray(np.asarray(inp["L1_ffn_b"], dtype=np.float32))
    w["vr"] = np.ascontiguousarray(np.asarray(inp["L1_ffn_V_r"], dtype=np.float32))
    w["cT"] = np.ascontiguousarray(np.asarray(inp["L1_ffn_C"], dtype=np.float32).T)
    return w


def make_maps(inputs, consts=None, w=None):
    consts = consts or _host_consts()
    w = w or _prep_weights(inputs)
    x = np.asarray(inputs["L0_attn_x"], dtype=np.float32)
    shared0 = dict(wqkvT=w["wqkvT"], wguT=w["wguT"], wdT=w["wdT"],
                   woT=w["woT"], cosq=consts["cosq"], sinq=consts["sinq"],
                   cosk=consts["cosk"], sink=consts["sink"],
                   ind_den=consts["ind_den"], ind3=consts["ind3"],
                   fconst=consts["fconst"])
    shared1 = dict(wqkv1=w["wqkv1"], bqkv1=w["bqkv1"], wo1=w["wo1"], bo1=w["bo1"],
                   gam=w["gam"], bet=w["bet"], w1T=w["w1T"], b1=w["b1"],
                   vr=w["vr"], cT=w["cT"], indH=consts["indH"],
                   ind32=consts["ind32"], ind32T=consts["ind32T"],
                   ident=consts["ident"], fconst=consts["fconst"])
    maps0 = [dict(shared0, x=np.ascontiguousarray(x[b].T)) for b in range(8)]
    maps1 = [dict(shared1) for b in range(8)]
    return maps0, maps1


def kernel(**inputs):
    maps0, maps1 = make_maps(inputs)
    maps = [dict(maps0[b], **maps1[b]) for b in range(8)]
    nc, es = build_full()
    r = run_bass_kernel_spmd(nc, maps, core_ids=list(range(8)))
    out = np.stack([np.asarray(r.results[b]["y"]).T for b in range(8)], 0)
    return np.ascontiguousarray(out).astype(np.float32)
